# version 24
# baseline (speedup 1.0000x reference)
import contextlib
import numpy as np
import ml_dtypes
import concourse.bass as bass
import concourse.mybir as mybir
from concourse.bass_utils import run_bass_kernel_spmd

F32 = mybir.dt.float32
BF16 = mybir.dt.bfloat16
AF = mybir.ActivationFunctionType
ALU = mybir.AluOpType
AX = mybir.AxisListType

T, D, H = 4096, 1024, 8
TILE, G = 512, 4
FBN = 32
SCALE = 128.0 ** -0.5
EPS = 1e-5

DBG_TILES = None
DBG_CORES = None
DBG_DUMP = False
MERGE_MAIN = True


class Prog:
    ENGINES = ("sync", "tensor", "vector", "scalar", "gpsimd")

    def __init__(self, nc, es):
        self.nc = nc
        self.blocks = []
        self.cur = None
        self.events = {}
        self.sems = {}
        self._es = es
        for name in ("pe", "v", "s", "p", "dLD", "dW", "dOUT", "dPO"):
            h = es.enter_context(nc.semaphore("sem_" + name))
            self.sems[name] = [h, 0]
        self.eng_sem = {"tensor": "pe", "vector": "v", "scalar": "s", "gpsimd": "p"}

    def dma_sem(self, name):
        if name not in self.sems:
            h = self._es.enter_context(self.nc.semaphore("sem_" + name))
            self.sems[name] = [h, 0]
        return name

    def open_block(self):
        self.cur = {e: [] for e in self.ENGINES}

    def close_block(self):
        self.blocks.append(self.cur)
        self.cur = None

    @contextlib.contextmanager
    def block(self):
        self.open_block()
        yield
        self.close_block()

    _anon = [0]

    def op(self, eng, fn, deps=(), event=None, dma=None, fence=False):
        deps = list(deps)
        pend = getattr(self, "pending", None)
        if pend is None:
            self.pending = pend = {}
        if pend.get(eng):
            deps.append(pend.pop(eng))
        inc = None
        if dma is not None:
            s = self.sems[dma]
            s[1] += 16
            inc = (dma, 16)
            if event:
                self.events[event] = (dma, s[1], eng)
        elif event or fence:
            if event is None:
                self._anon[0] += 1
                event = f"_anon{self._anon[0]}"
            sname = self.eng_sem[eng]
            s = self.sems[sname]
            s[1] += 1
            inc = (sname, 1)
            self.events[event] = (sname, s[1], eng)
        if fence:
            self.pending[eng] = (self.eng_sem[eng], self.sems[self.eng_sem[eng]][1])
        self.cur[eng].append((fn, tuple(deps), inc))

    def ev(self, name):
        sname, val, _ = self.events[name]
        return (sname, val)

    def build(self):
        clear_blk = {e: [] for e in self.ENGINES}
        for name in list(self.sems):
            h = self.sems[name][0]
            clear_blk["sync"].append((lambda y, h=h: y.sem_clear(h), (), None))
        self.blocks.insert(0, clear_blk)
        waited = {}
        for blk in self.blocks:
            with self.nc.Block() as block:
                for ename in self.ENGINES:
                    ops = blk[ename]
                    if not ops:
                        continue

                    def body(engine, ops=ops, ename=ename):
                        for fn, deps, inc in ops:
                            need = {}
                            for dep in deps:
                                if isinstance(dep, str):
                                    sname, val, producer = self.events[dep]
                                else:
                                    sname, val = dep
                                if need.get(sname, 0) < val:
                                    need[sname] = val
                            for sname, val in need.items():
                                key = (ename, sname)
                                if waited.get(key, 0) < val:
                                    engine.wait_ge(self.sems[sname][0], val)
                                    waited[key] = val
                            ins = fn(engine)
                            if inc is not None:
                                ins.then_inc(self.sems[inc[0]][0], inc[1])

                    getattr(block, ename)(body)


def build_nc(n_tiles):
    nc = bass.Bass("TRN2", target_bir_lowering=False)
    din = lambda name, shape, dt=F32: nc.dram_tensor(name, shape, dt, kind="ExternalInput").ap()
    dsc = lambda name, shape, dt=BF16: nc.dram_tensor(name, shape, dt, kind="Internal").ap()

    x_d = din("x", [T, D])
    wT_d = [din(n, [128, 8, 1024]) for n in ("wqT", "wkT", "wvT", "woT")]
    wiT_d = din("wiT", [FBN, 128, 1024])
    dnT_d = din("dnT", [FBN, 128, 1024])
    ballT_d = din("ballT", [128, 4096])
    aallT_d = din("aallT", [128, 1024])
    gwT_d = din("gwT", [128, 64])
    fm_d = [din(n, [128, 128]) for n in ("fmq1", "fmq2", "fmk1", "fmk2")]
    fbq1_d = din("fbq1", [128, 1]); fbk1_d = din("fbk1", [128, 1])
    fbq2_d = din("fbq2", [1, 128]); fbk2_d = din("fbk2", [1, 128])
    bi_d = din("bi", [128, 32])
    bo_d = din("bo", [1, 1024])
    gb_d = din("gb", [1, 32])
    rms_d = din("rms", [128, 1])
    identb_d = din("identb", [128, 128], BF16)
    mask_d = din("mask", [128, 128])
    eexp_d = din("eexp", [8, 128], BF16)
    ones_d = din("ones", [1, 512], BF16)
    out_d = nc.dram_tensor("out", [T, D], F32, kind="ExternalOutput").ap()

    wT_s = [dsc(n, [128, 8, 1024]) for n in ("wqT_s", "wkT_s", "wvT_s", "woT_s")]
    wiT_s = dsc("wiT_s", [FBN, 128, 1024])
    dnT_s = dsc("dnT_s", [FBN, 128, 1024])

    es = contextlib.ExitStack()
    sb = lambda name, shape, dt: es.enter_context(nc.sbuf_tensor(name, shape, dt))

    ball_r = sb("ball_r", [128, 4096], BF16)
    aall_r = sb("aall_r", [128, 1024], BF16)
    gw_r = sb("gw_r", [128, 64], BF16)
    fm_r = [sb(n, [128, 128], BF16) for n in ("fmq1_r", "fmq2_r", "fmk1_r", "fmk2_r")]
    fbq1_r = sb("fbq1_r", [128, 1], F32)
    fbk1_r = sb("fbk1_r", [128, 1], F32)
    fbq2_row = sb("fbq2_row", [1, 128], BF16)
    fbk2_row = sb("fbk2_row", [1, 128], BF16)
    gb_row = sb("gb_row", [1, 32], BF16)
    bo2_row = sb("bo2_row", [1, 1024], BF16)
    bi_r = sb("bi_r", [128, 32], F32)
    gb_bc = sb("gb_bc", [128, 32], F32)
    rms_r = sb("rms_r", [128, 1], F32)
    eps_c = sb("eps_c", [128, 1], F32)
    identb_r = sb("identb_r", [128, 128], BF16)
    mask_r = sb("mask_r", [128, 128], F32)
    eexp_r = sb("eexp_r", [8, 128], BF16)
    ones_r = sb("ones_r", [1, 512], BF16)
    x_sb = sb("x_sb", [128, G, 1024], F32)
    xng = sb("xng", [128, 4, 1024], BF16)
    xnT = sb("xnT", [128, 2, 8, 512], BF16)
    xF = sb("xF", [128, 2, 1024], F32)

    ps = es.enter_context(nc.psum_tensor("ps", [128, 8, 512], F32))
    bank = lambda b: ps[:, b, :]

    def carve(b, idx):
        return ps[:, b, idx * 64:(idx + 1) * 64].bitcast(BF16)

    def carve8(b, idx):
        return ps[0:8, b, idx * 64:(idx + 1) * 64].bitcast(BF16)

    P = Prog(nc, es)

    with es:
        xv = x_d.rearrange("(t g p) d -> t p g d", p=128, g=G)
        ov = out_d.rearrange("(t g p) d -> t p g d", p=128, g=G)

        with P.block():
            loads = [(identb_r[:], identb_d), (mask_r[:], mask_d),
                     (eexp_r[:], eexp_d), (ones_r[:], ones_d), (fbq1_r[:], fbq1_d),
                     (fbk1_r[:], fbk1_d), (bi_r[:], bi_d), (rms_r[:], rms_d),
                     (x_sb[0:1, 0, 0:1024], bo_d), (x_sb[:, 1, 0:128], fm_d[0])]
            for i, (dst, src_) in enumerate(loads):
                P.op("sync", lambda y, d=dst, s=src_: y.dma_start(out=d, in_=s),
                     dma="dLD", event=f"p0ld{i}")
            allld = [f"p0ld{i}" for i in range(len(loads))]
            for nm, dst, src_ in [("cfm1", fm_r[1][:], fm_d[1]), ("cfm2", fm_r[2][:], fm_d[2]),
                                  ("cfm3", fm_r[3][:], fm_d[3]),
                                  ("cfq2", fbq2_row[:], fbq2_d), ("cfk2", fbk2_row[:], fbk2_d),
                                  ("cgb", gb_row[:], gb_d),
                                  ("caall", aall_r[:], aallT_d), ("cgw", gw_r[:], gwT_d),
                                  ("cball", ball_r[:], ballT_d)]:
                P.op("gpsimd", lambda y, d=dst, s=src_: y.dma_start(out=d, in_=s),
                     dma=P.dma_sem("d" + nm), event=nm)
            smalls = ["cfm1", "cfm2", "cfm3", "cfq2", "cfk2", "cgb", "caall", "cgw"]
            P.op("vector", lambda v: v.memset(eps_c[:], EPS), deps=allld + smalls)
            P.op("vector", lambda v: v.tensor_scalar(fm_r[0][:], x_sb[:, 1, 0:128], SCALE, None, ALU.mult))
            P.op("vector", lambda v: v.tensor_scalar(fbq1_r[:], fbq1_r[:], SCALE, None, ALU.mult))
            P.op("vector", lambda v: v.tensor_scalar(bo2_row[:], x_sb[0:1, 0, 0:1024], 2.0, None, ALU.mult))
            P.op("vector", lambda v: v.memset(state[:], 0.0), event="p0v")
            for g in range(G):
                P.op("gpsimd", lambda y, g=g: y.dma_start(out=x_sb[:, g, :], in_=xv[0, :, g, :]),
                     deps=["p0v"], dma=P.dma_sem(f"dX{g}"), event=f"t0xld{g}")
            for nm, dst, src_ in [("cwq", wT_s[0], wT_d[0]), ("cwk", wT_s[1], wT_d[1]),
                                  ("cwv", wT_s[2], wT_d[2]), ("cwo", wT_s[3], wT_d[3]),
                                  ("cwi", wiT_s.rearrange("f p j -> (f p) j"),
                                   wiT_d.rearrange("f p j -> (f p) j")),
                                  ("cdn", dnT_s.rearrange("f p j -> (f p) j"),
                                   dnT_d.rearrange("f p j -> (f p) j"))]:
                P.op("gpsimd", lambda y, d=dst, s=src_: y.dma_start(out=d, in_=s),
                     dma=P.dma_sem("d" + nm), event=nm)
            P.op("tensor", lambda t: t.matmul(ps[:, 0, 0:32], ones_r[0:1, 0:128], gb_row[:],
                                              start=True, stop=True), deps=["p0v", "cgb"], event="p0mm")
            P.op("scalar", lambda s: s.copy(gb_bc[:], ps[:, 0, 0:32]), deps=["p0mm"], event="p0gb")


        wbuf = sb("wbuf", [128, 2, 8, 512], BF16)
        q0T = sb("q0T", [128, 2, 512], BF16)
        scr512 = sb("scr512", [128, 2, 512], BF16)
        qT = sb("qT", [128, 8, 512], BF16)
        kT = sb("kT", [128, 8, 512], BF16)
        ktok = sb("ktok", [128, 8, 128], BF16)
        v_sb = sb("v_sb", [128, G, 1024], BF16)
        sbf = sb("sbf", [128, 8, 128], BF16)
        state = sb("state", [128, 1024], F32)
        atm = sb("atm", [128, 2, 128], BF16)
        o_n = sb("o_n", [128, G, 1024], BF16)
        onT = sb("onT", [128, 8, 128], BF16)
        s_sb = sb("s_sb", [128, G, 1024], BF16)
        zT_sb = sb("zT_sb", [128, 512], BF16)
        zm1 = sb("zm1", [128, 512], BF16)
        zdf = sb("zdf", [128, 512], BF16)
        L_sb = sb("L_sb", [128, 4, 8], F32)
        L2_sb = sb("L2_sb", [128, 8], F32)
        eq1_b = sb("eq1_b", [128, 8], BF16)
        eq1g = sb("eq1g", [128, 4, 8], BF16)
        eqdg = sb("eqdg", [128, 4, 8], BF16)
        eq2_b = sb("eq2_b", [128, 8], BF16)
        eqd_b = sb("eqd_b", [128, 8], BF16)
        eq1T = sb("eq1T", [8, 512], BF16)
        eqdT = sb("eqdT", [8, 512], BF16)
        vsum = sb("vsum", [128, G], F32)
        sc1 = sb("sc1", [128, 16], F32)
        rst8 = sb("rst8", [128, 8], F32)
        hs = sb("hs", [128, FBN, 512], BF16)
        wiring = sb("wiring", [128, 3, 1024], BF16)
        dnring = sb("dnring", [128, 3, 1024], BF16)
        r1r = sb("r1r", [128, 2, 512], BF16)
        t2r = sb("t2r", [128, 2, 512], BF16)
        g_sb = sb("g_sb", [128, 1024], F32)
        outr = sb("outr", [128, 2, 1024], F32)
        junk = sb("junk", [128, 1024], BF16)
        dbgsc = sb("dbgsc", [128, 64], F32)
        dbge = sb("dbge", [128, 32], F32)

        if DBG_DUMP:
            dsall = nc.dram_tensor("d_s_all", [T // TILE, 128, G, 1024], BF16,
                                   kind="ExternalOutput").ap()
            dgall = nc.dram_tensor("d_g_all", [T // TILE, 128, G, 1024], F32,
                                   kind="ExternalOutput").ap()
            dhall = nc.dram_tensor("d_hs_all", [T // TILE, 128, FBN, 512], BF16,
                                   kind="ExternalOutput").ap()
            dvall = nc.dram_tensor("d_vs_all", [T // TILE, 128, G], F32,
                                   kind="ExternalOutput").ap()

        wb_ev = [None, None]
        wb_ld = [None, None]
        wi_ld = [None] * 3
        wi_fr = [None] * 3
        dn_ld = [None] * 3
        dn_fr = [None] * 3
        x_ld = [f"t0xld{g}" for g in range(G)]
        last_store = [None, None]
        nwb = [0]

        def issue_wb(widx, half, slot, deps=()):
            nwb[0] += 1
            name = f"wb{nwb[0]}"
            d = list(deps) + [["cwq", "cwk", "cwv", "cwo"][widx]]
            if wb_ev[slot]:
                d.append(wb_ev[slot])
            P.op("sync", lambda y, w=widx, h=half, sl=slot: y.dma_start(
                out=wbuf[:, sl, :, :], in_=wT_s[w][:, :, h * 512:(h + 1) * 512]),
                 deps=d, dma=P.dma_sem(f"dWB{slot}"), event=name)
            wb_ld[slot] = name

        def issue_wi(t, fb):
            slot = fb % 3
            name = f"wi{t}_{fb}"
            d = [wi_fr[slot]] if wi_fr[slot] else ["cwi"]
            P.op("sync", lambda y, fb=fb, sl=slot: y.dma_start(
                out=wiring[:, sl, :], in_=wiT_s[fb]), deps=d, dma=P.dma_sem(f"dWI{slot}"), event=name)
            wi_ld[slot] = name

        def issue_dn(t, fb):
            slot = fb % 3
            name = f"dn{t}_{fb}"
            d = [dn_fr[slot]] if dn_fr[slot] else ["cdn"]
            P.op("sync", lambda y, fb=fb, sl=slot: y.dma_start(
                out=dnring[:, sl, :], in_=dnT_s[fb]), deps=d, dma=P.dma_sem(f"dDN{slot}"), event=name)
            dn_ld[slot] = name


        with P.block():
            P.op("vector", lambda v: v.memset(state[:], 0.0), event="st0")

        phase_blk = contextlib.nullcontext if MERGE_MAIN else P.block
        if MERGE_MAIN:
            P.open_block()
        def emit_A_dma(t):
            with phase_blk():
                for g in range(G):
                    if t > 0:
                        P.op("gpsimd", lambda y, g=g, t=t: y.dma_start(
                            out=x_sb[:, g, :], in_=xv[t, :, g, :]),
                             deps=[f"t{t-1}xn{g}", f"t{t-1}rstd{g}"], dma=P.dma_sem(f"dX{g}"),
                             event=f"t{t}xld{g}")
                    x_ld[g] = f"t{t}xld{g}"

        def emit_A(t):
            with phase_blk():
                for g in range(G):
                    emit_A_group(t, g)

        def emit_A_group(t, g):
            tn = f"t{t}"
            if True:
                if True:
                    gd = [x_ld[g]]
                    P.op("vector", lambda v, g=g: v.reduce_sum(sc1[:, g:g + 1], x_sb[:, g, :], axis=AX.X), deps=gd, fence=True)
                    P.op("vector", lambda v, g=g: v.tensor_scalar(
                        sc1[:, g:g + 1], sc1[:, g:g + 1], -1.0 / D, None, ALU.mult), event=f"{tn}nm{g}")
                    P.op("scalar", lambda s, g=g: s.activation(
                        junk[:], x_sb[:, g, :], AF.Square, bias=sc1[:, g:g + 1],
                        accum_out=sc1[:, 8 + g:9 + g]), deps=[f"{tn}nm{g}"] + gd)
                    P.op("scalar", lambda s, g=g: s.activation(
                        sc1[:, 8 + g:9 + g], sc1[:, 8 + g:9 + g], AF.Ln, scale=1.0 / D, bias=eps_c[:]))
                    P.op("scalar", lambda s, g=g: s.activation(
                        sc1[:, 8 + g:9 + g], sc1[:, 8 + g:9 + g], AF.Exp, scale=-0.5), event=f"{tn}rstd{g}")
                    xdeps = [f"{tn}rstd{g}"]
                    if t >= 1:
                        xdeps.append(f"t{t-1}xt{g}_pe")
                    P.op("vector", lambda v, g=g: v.tensor_scalar(
                        xng[:, g, :], x_sb[:, g, :], sc1[:, g:g + 1], sc1[:, 8 + g:9 + g],
                        ALU.add, ALU.mult), deps=xdeps, event=f"{tn}xn{g}")

        def emit_A_tr(t):
            tn = f"t{t}"
            tp = t % 2
            with phase_blk():
                for g in range(G):
                    trdeps = [f"{tn}xn{g}"]
                    if g >= 1:
                        trdeps += [f"{tn}xe{g-1}_0", f"{tn}xe{g-1}_1"]
                    elif t >= 1:
                        trdeps.append(f"t{t-1}fsi0")
                    else:
                        trdeps.append("p0gb")
                    for c in range(8):
                        b, q = c // 4, c % 4
                        ev = f"{tn}xt{g}_pe" if c == 7 else None
                        P.op("tensor", lambda tr, g=g, c=c, b=b, q=q: tr.transpose(
                            carve(b, q), xng[:, g, c * 128:(c + 1) * 128],
                            identb_r[:]), deps=trdeps if c == 0 else (), event=ev)
                    for b in range(2):
                        xedeps = [f"{tn}xt{g}_pe"]
                        P.op("vector", lambda v, g=g, b=b, tp=tp: v.tensor_copy(
                            xnT[:, tp, b * 4:(b + 1) * 4, g * 128:(g + 1) * 128],
                            ps[:, b, 0:256].bitcast(BF16).rearrange("p (q n) -> p q n", q=4)),
                             deps=xedeps, event=f"{tn}xe{g}_{b}")

        for t in range(n_tiles):
            tn = f"t{t}"
            tp = t % 2
            xnT_done = [f"{tn}xe{g}_{b}" for g in range(G) for b in range(2)]
            if t == 0:
                emit_A(0)
                issue_wb(0, 0, 0)
                issue_wb(0, 1, 1)
            emit_A_tr(t)
            if t + 1 < n_tiles:
                emit_A_dma(t + 1)

            with phase_blk():
                for wi_, dstT, w1, w2, b1c, b2row in [
                        (0, qT, fm_r[0], fm_r[1], fbq1_r, fbq2_row),
                        (1, kT, fm_r[2], fm_r[3], fbk1_r, fbk2_row)]:
                    wn = f"{tn}w{wi_}"
                    pwn = f"{tn}w0"

                    def prev(h, suffix):
                        if h >= 2:
                            return [f"{wn}{suffix}{h-2}"]
                        if wi_ == 1:
                            return [f"{pwn}{suffix}{h+6}"]
                        if t >= 1:
                            return [f"t{t-1}w1{suffix}{h+6}"]
                        return []

                    for h in range(H):
                        half = h // 4
                        qb = h % 2
                        deps = [wb_ld[half]] + xnT_done + prev(h, "qe")
                        if wi_ == 0 and h < 2:
                            deps += [f"{tn}xe3_0", f"{tn}xe3_1"]
                        for c in range(8):
                            ev = f"{wn}q{h}" if c == 7 else None
                            P.op("tensor", lambda tr, h=h, c=c, qb=qb, half=half, tp=tp: tr.matmul(
                                bank(qb), wbuf[:, half, c, (h % 4) * 128:(h % 4 + 1) * 128],
                                xnT[:, tp, c, :], start=(c == 0), stop=(c == 7)),
                                 deps=deps if c == 0 else (), event=ev)
                        qed = [f"{wn}q{h}"] + prev(h, "fm")
                        P.op("vector", lambda v, h=h, qb=qb: v.tensor_copy(
                            q0T[:, h % 2, :], bank(qb)), deps=qed, event=f"{wn}qe{h}")
                        f1, f2 = 2 + h % 2, 4 + h % 2
                        d1 = [f"{wn}qe{h}"] + prev(h, "t1_")
                        if wi_ == 0 and h < 2 and t >= 1:
                            d1.append(f"t{t-1}fsi1")
                        P.op("tensor", lambda tr, h=h, f1=f1, w1=w1: tr.matmul(
                            bank(f1), w1[:], q0T[:, h % 2, :], start=True, stop=True), deps=d1)
                        d2 = [f"{wn}qe{h}"] + prev(h, "qT")
                        if wi_ == 0 and h < 2 and t >= 1:
                            d2.append(f"t{t-1}fsi2")
                        P.op("tensor", lambda tr, h=h, f2=f2, w2=w2: tr.matmul(
                            bank(f2), w2[:], q0T[:, h % 2, :], start=True, stop=False), deps=d2)
                        P.op("tensor", lambda tr, h=h, f2=f2, b2row=b2row: tr.matmul(
                            bank(f2), b2row[:], ones_r[:], start=False, stop=True),
                             event=f"{wn}fm{h}")
                        t1deps = [f"{wn}fm{h}"] + prev(h, "qT")
                        if wi_ == 0 and h < 2 and t >= 1:
                            t1deps.append(f"t{t-1}e{30 + h % 2}t2")
                        P.op("scalar", lambda s, h=h, f1=f1, b1c=b1c: s.activation(
                            scr512[:, h % 2, :], bank(f1), AF.Identity, bias=b1c[:]),
                             deps=t1deps, event=f"{wn}t1_{h}")
                        qTdeps = [f"{wn}t1_{h}"]
                        if t >= 1:
                            qTdeps.append(f"t{t-1}g3o{h}")
                            if wi_ == 1:
                                qTdeps.append(f"t{t-1}g3ktr{h // 4}")
                        P.op("vector", lambda v, h=h, f2=f2, dstT=dstT: v.tensor_tensor(
                            dstT[:, h, :], bank(f2), scr512[:, h % 2, :], ALU.mult),
                             deps=qTdeps, event=f"{wn}qT{h}")
                    wb_ev[0] = f"{wn}q3"
                    wb_ev[1] = f"{wn}q7"
                    if wi_ == 0:
                        issue_wb(1, 0, 0)
                        issue_wb(1, 1, 1)
                    else:
                        issue_wb(2, 0, 0)
                        issue_wb(2, 1, 1)

                vfree = [f"{tn}w1qe6", f"{tn}w1qe7", f"{tn}w1t1_6", f"{tn}w1t1_7"]
                for half in range(2):
                    for g in range(G):
                        vb = 2 * (g % 2) + half
                        deps = [wb_ld[half]] + vfree
                        if g >= 2:
                            deps.append(f"{tn}ve{g-2}_{half}")
                        for c in range(8):
                            ev = f"{tn}vm{g}_{half}" if c == 7 else None
                            P.op("tensor", lambda tr, g=g, c=c, vb=vb, half=half, tp=tp: tr.matmul(
                                bank(vb), xnT[:, tp, c, g * 128:(g + 1) * 128],
                                wbuf[:, half, c, :], start=(c == 0), stop=(c == 7)),
                                 deps=deps if c == 0 else (), event=ev)
                        vedeps = [f"{tn}vm{g}_{half}"]
                        if t >= 1:
                            vedeps += [f"t{t-1}g{g}kv0", f"t{t-1}g{g}kv1"]
                        P.op("vector", lambda v, g=g, half=half, vb=vb: v.tensor_copy(
                            v_sb[:, g, half * 512:(half + 1) * 512], bank(vb)),
                             deps=vedeps, event=f"{tn}ve{g}_{half}")
                wb_ev[0] = f"{tn}vm3_0"
                wb_ev[1] = f"{tn}vm3_1"
                issue_wb(3, 0, 0)
                issue_wb(3, 1, 1)

                zdeps = xnT_done + [f"{tn}w1qT6", f"{tn}w1qT7"]
                if t >= 1:
                    zdeps.append(f"t{t-1}fsi3")
                for c in range(8):
                    P.op("tensor", lambda tr, c=c, tp=tp: tr.matmul(
                        bank(6), aall_r[:, c * 128:(c + 1) * 128], xnT[:, tp, c, :],
                        start=(c == 0), stop=(c == 7)),
                         deps=zdeps if c == 0 else (), event=f"{tn}zmm" if c == 7 else None)
                zedeps = [f"{tn}zmm"]
                if t >= 1:
                    zedeps += [f"t{t-1}e{FBN-1}p1", f"t{t-1}e{FBN-1}pd"]
                P.op("vector", lambda v: v.tensor_copy(zT_sb[:], bank(6)), deps=zedeps,
                     event=f"{tn}ze")

                lg0deps = xnT_done + ([f"t{t-1}fsi3", f"t{t-1}m1mm", f"t{t-1}mdmm"] if t >= 1 else [])
                for g in range(G):
                    for c in range(8):
                        ev = f"{tn}lgall" if (g == G - 1 and c == 7) else None
                        P.op("tensor", lambda tr, g=g, c=c, tp=tp: tr.matmul(
                            ps[:, 7, g * 8:(g + 1) * 8], xnT[:, tp, c, g * 128:(g + 1) * 128],
                            gw_r[:, c * 8:(c + 1) * 8], start=(c == 0), stop=(c == 7)),
                             deps=lg0deps if (g == 0 and c == 0) else (), event=ev)
                Ldeps = [f"{tn}lgall", "p0gb"]
                if t >= 1:
                    Ldeps.append(f"t{t-1}gex3")
                P.op("vector", lambda v: v.tensor_tensor(
                    L_sb[:].rearrange("p a e -> p (a e)"), ps[:, 7, 0:32], gb_bc[:], ALU.add),
                     deps=Ldeps, event=f"{tn}Leall", fence=True)
            with phase_blk():
                for g in range(G):
                    cg = t * G + g
                    gn = f"{tn}g{g}"
                    pgn = f"{tn}g{g-1}" if g >= 1 else f"t{t-1}g{G-1}"
                    for bb in range(2):
                        tdeps = []
                        if cg >= 1:
                            tdeps.append(f"{pgn}tke{bb}")
                        if g == 0:
                            tdeps += [f"{tn}ze", f"{tn}Leall"]
                        for hh in range(4):
                            h = bb * 4 + hh
                            ev = f"{gn}ktr{bb}" if hh == 3 else None
                            P.op("tensor", lambda tr, h=h, bb=bb, hh=hh, g=g: tr.transpose(
                                carve(6 + bb, hh), kT[:, h, g * 128:(g + 1) * 128], identb_r[:]),
                                 deps=(tdeps + [f"{tn}w1qT{h}"]) if hh == 0 else (), event=ev)
                        P.op("vector", lambda v, bb=bb, g=g: v.tensor_copy(
                            ktok[:, bb * 4:(bb + 1) * 4, :],
                            ps[:, 6 + bb, 0:256].bitcast(BF16).rearrange("p (q n) -> p q n", q=4)),
                             deps=[f"{gn}ktr{bb}"], event=f"{gn}tke{bb}")
                    for h in range(H):
                        ob, ab = 2 + h % 2, 4 + h % 2
                        hp = h - 2 if h >= 2 else h + 6
                        pn = gn if h >= 2 else pgn
                        adeps = [f"{tn}w1qT{h}", f"{tn}w0qT{h}"]
                        if cg >= 1 or h >= 2:
                            adeps.append(f"{pn}atm{hp}")
                        if g == 0 and h < 2:
                            adeps += [f"{tn}w1qT6", f"{tn}w1qT7"]
                        P.op("tensor", lambda tr, h=h, g=g, ab=ab: tr.matmul(
                            bank(ab)[:, 0:128], kT[:, h, g * 128:(g + 1) * 128],
                            qT[:, h, g * 128:(g + 1) * 128], start=True, stop=True),
                             deps=adeps, event=f"{gn}A{h}")
                        amdeps = [f"{gn}A{h}"]
                        if cg >= 1 or h >= 2:
                            amdeps.append(f"{pn}o{hp}")
                        P.op("vector", lambda v, h=h, ab=ab: v.tensor_tensor(
                            atm[:, h % 2, :], bank(ab)[:, 0:128], mask_r[:], ALU.mult),
                             deps=amdeps, event=f"{gn}atm{h}")
                        odeps = []
                        if cg >= 1 or h >= 2:
                            odeps += [f"{pn}oe{hp}", f"{pn}sq{hp}"]
                        if g == 0 and h < 2:
                            odeps += [f"{tn}ve3_0", f"{tn}ve3_1"]
                        if cg > 0:
                            P.op("tensor", lambda tr, h=h, g=g, ob=ob: tr.matmul(
                                bank(ob)[:, 0:128], qT[:, h, g * 128:(g + 1) * 128],
                                sbf[:, h, :], start=True, stop=False),
                                 deps=odeps + [f"{pgn}cast{h}"])
                            P.op("tensor", lambda tr, h=h, g=g, ob=ob: tr.matmul(
                                bank(ob)[:, 0:128], atm[:, h % 2, :], v_sb[:, g, h * 128:(h + 1) * 128],
                                start=False, stop=True),
                                 deps=[f"{gn}atm{h}", f"{tn}ve{g}_{h // 4}"], event=f"{gn}o{h}")
                        else:
                            P.op("tensor", lambda tr, h=h, g=g, ob=ob: tr.matmul(
                                bank(ob)[:, 0:128], atm[:, h % 2, :], v_sb[:, g, h * 128:(h + 1) * 128],
                                start=True, stop=True),
                                 deps=odeps + [f"{gn}atm{h}", f"{tn}ve{g}_{h // 4}"], event=f"{gn}o{h}")
                        sqdeps = [f"{gn}o{h}"]
                        if cg >= 1:
                            sqdeps.append(f"{pgn}oe{h}")
                        P.op("scalar", lambda s, h=h, ob=ob: s.activation(
                            junk[:, 0:128], bank(ob)[:, 0:128], AF.Square,
                            accum_out=rst8[:, h:h + 1]), deps=sqdeps)
                        P.op("scalar", lambda s, h=h: s.activation(
                            rst8[:, h:h + 1], rst8[:, h:h + 1], AF.Ln, scale=1.0 / 128, bias=eps_c[:]))
                        P.op("scalar", lambda s, h=h: s.activation(
                            rst8[:, h:h + 1], rst8[:, h:h + 1], AF.Exp, scale=-0.5),
                             event=f"{gn}sq{h}")
                    
                        oedeps = [f"{gn}sq{h}"]
                        if t >= 1:
                            oedeps.append(f"t{t-1}d{g}otr{h // 4}")
                        P.op("vector", lambda v, h=h, g=g, ob=ob: v.tensor_scalar(
                            o_n[:, g, h * 128:(h + 1) * 128], bank(ob)[:, 0:128],
                            rst8[:, h:h + 1], None, ALU.mult), deps=oedeps, event=f"{gn}oe{h}")
                    for bb in range(2):
                        kdeps = [f"{gn}tke{bb}"]
                        if cg >= 1:
                            kdeps.append(f"{pgn}cast{bb * 4 + 3}")
                        if g == 0:
                            kdeps += [f"{tn}ve2_0", f"{tn}ve2_1"]
                        for hh in range(4):
                            h = bb * 4 + hh
                            ev = f"{gn}kv{bb}" if hh == 3 else None
                            P.op("tensor", lambda tr, h=h, hh=hh, bb=bb, g=g: tr.matmul(
                                ps[:, bb, hh * 128:(hh + 1) * 128], ktok[:, h, :],
                                v_sb[:, g, h * 128:(h + 1) * 128], start=(hh == 0), stop=(hh == 3),
                                skip_group_check=True),
                                 deps=kdeps if hh == 0 else (), event=ev)
                    for bb in range(2):
                        for hh in range(4):
                            h = bb * 4 + hh
                            adddeps = [f"{gn}kv{bb}"]
                            if cg >= 1:
                                adddeps.append(f"{pgn}cast{h}")
                            if cg == 0:
                                P.op("vector", lambda v, h=h, hh=hh, bb=bb: v.tensor_copy(
                                    state[:, h * 128:(h + 1) * 128], ps[:, bb, hh * 128:(hh + 1) * 128]),
                                     deps=adddeps, event=f"{gn}add{h}")
                            else:
                                P.op("vector", lambda v, h=h, hh=hh, bb=bb: v.tensor_tensor(
                                    state[:, h * 128:(h + 1) * 128], state[:, h * 128:(h + 1) * 128],
                                    ps[:, bb, hh * 128:(hh + 1) * 128], ALU.add),
                                     deps=adddeps, event=f"{gn}add{h}")
                            P.op("scalar", lambda s, h=h: s.copy(
                                sbf[:, h, :], state[:, h * 128:(h + 1) * 128]),
                                 deps=[f"{gn}add{h}", f"{gn}o{h}"], event=f"{gn}cast{h}")
                    if True:
                        Lg = L_sb[:, g, :]
                        P.op("vector", lambda v, Lg=Lg: v.reduce_max(sc1[:, 12:13], Lg, axis=AX.X), fence=True)
                        P.op("vector", lambda v, Lg=Lg: v.tensor_scalar(eq1_b[:], Lg, sc1[:, 12:13], None, ALU.is_equal),
                             fence=True)
                        P.op("vector", lambda v, Lg=Lg: v.scalar_tensor_tensor(
                            L2_sb[:], eq1_b[:], -1e30, Lg, ALU.mult, ALU.add), fence=True)
                        P.op("vector", lambda v: v.reduce_max(sc1[:, 13:14], L2_sb[:], axis=AX.X), fence=True)
                        P.op("vector", lambda v: v.tensor_scalar(eq2_b[:], L2_sb[:], sc1[:, 13:14], None, ALU.is_equal), fence=True)
                        P.op("vector", lambda v: v.tensor_tensor(eqd_b[:], eq2_b[:], eq1_b[:], ALU.subtract),
                             event=f"{tn}eqd{g}")
                        P.op("vector", lambda v: v.tensor_scalar(sc1[:, 14:15], sc1[:, 12:13], -1.0, None, ALU.mult),
                             event=f"{tn}gnm{g}")
                        P.op("scalar", lambda s, Lg=Lg: s.activation(L2_sb[:], Lg, AF.Exp,
                                                              bias=sc1[:, 14:15]),
                             deps=[f"{tn}gnm{g}", f"{tn}eqd{g}"], event=f"{tn}gex{g}")
                        P.op("vector", lambda v: v.reduce_sum(sc1[:, 15:16], L2_sb[:], axis=AX.X),
                             deps=[f"{tn}gex{g}"], fence=True)
                        P.op("vector", lambda v: v.tensor_tensor(sc1[:, 13:14], sc1[:, 13:14],
                                                                 sc1[:, 12:13], ALU.subtract),
                             event=f"{tn}gdm{g}")
                        P.op("scalar", lambda s: s.activation(sc1[:, 12:13], sc1[:, 13:14], AF.Exp),
                             deps=[f"{tn}gdm{g}"], event=f"{tn}ge2{g}")
                        P.op("vector", lambda v: v.reciprocal(sc1[:, 15:16], sc1[:, 15:16]), fence=True)
                        vsdeps = [f"{tn}ge2{g}"]
                        if t >= 1:
                            vsdeps.append(f"t{t-1}fsi{g}")
                        if DBG_DUMP and g == 0:
                            vsdeps.append(("dOUT", P.sems["dOUT"][1]))
                        P.op("vector", lambda v, g=g: v.scalar_tensor_tensor(
                            vsum[:, g:g + 1], sc1[:, 12:13], 1.0, sc1[:, 15:16], ALU.add, ALU.mult),
                             deps=vsdeps, event=f"{tn}vs{g}")
                        eqsdeps = [f"t{t-1}et1{g}", f"t{t-1}et2{g}"] if t >= 1 else []
                        P.op("vector", lambda v, g=g: v.tensor_copy(eq1g[:, g, :], eq1_b[:]),
                             deps=eqsdeps, event=f"{tn}eqs1{g}")
                        P.op("vector", lambda v, g=g: v.tensor_copy(eqdg[:, g, :], eqd_b[:]),
                             event=f"{tn}eqs2{g}")


            with phase_blk():
                for g in range(G):
                    tdeps = [f"{tn}eqs1{g}", f"{tn}eqs2{g}", f"{tn}g3atm7"]
                    if g >= 1:
                        tdeps += [f"{tn}ee1{g-1}", f"{tn}ee2{g-1}"]
                    P.op("tensor", lambda tr, g=g: tr.transpose(carve8(5, 0), eq1g[:, g, :], identb_r[:]),
                         deps=tdeps, event=f"{tn}et1{g}")
                    P.op("tensor", lambda tr, g=g: tr.transpose(carve8(5, 1), eqdg[:, g, :], identb_r[:]),
                         event=f"{tn}et2{g}")
                    eedeps0 = [f"{tn}et1{g}"]
                    if g == 0 and t >= 1:
                        eedeps0 += [f"t{t-1}m1mm", f"t{t-1}mdmm"]
                    P.op("scalar", lambda s, g=g: s.copy(eq1T[:, g * 128:(g + 1) * 128], carve8(5, 0)),
                         deps=eedeps0, event=f"{tn}ee1{g}")
                    P.op("scalar", lambda s, g=g: s.copy(eqdT[:, g * 128:(g + 1) * 128], car8 if False else carve8(5, 1)),
                         deps=[f"{tn}et2{g}"], event=f"{tn}ee2{g}")
                eedeps = [f"{tn}ee1{g}" for g in range(G)] + [f"{tn}ee2{g}" for g in range(G)]
                P.op("tensor", lambda tr: tr.matmul(bank(4), eexp_r[:], eq1T[:], start=True, stop=True),
                     deps=eedeps + [f"{tn}ze", f"{tn}g3atm6"], event=f"{tn}m1mm")
                zm1deps = [f"{tn}m1mm"]
                if t >= 1:
                    zm1deps.append(f"t{t-1}e{FBN-1}p1")
                P.op("vector", lambda v: v.tensor_tensor(zm1[:], bank(4), zT_sb[:], ALU.mult),
                     deps=zm1deps, event=f"{tn}zm1")
                P.op("tensor", lambda tr: tr.matmul(bank(5), eexp_r[:], eqdT[:], start=True, stop=True),
                     deps=eedeps, event=f"{tn}mdmm")
                zdfdeps = [f"{tn}mdmm"]
                if t >= 1:
                    zdfdeps.append(f"t{t-1}e{FBN-1}pd")
                P.op("vector", lambda v: v.tensor_tensor(zdf[:], bank(5), zT_sb[:], ALU.mult),
                     deps=zdfdeps, event=f"{tn}zdf")


            with phase_blk():
                for g in range(G):
                    gn = f"{tn}d{g}"
                    trdeps = [f"{tn}g{g}oe{h}" for h in range(H)]
                    if g >= 1:
                        trdeps += [f"{tn}d{g-1}oTe0", f"{tn}d{g-1}oTe1"]
                    else:
                        trdeps += [f"{tn}g3tke0", f"{tn}g3tke1"]
                    for c in range(8):
                        bb, q = c // 4, c % 4
                        ev = f"{gn}otr{bb}" if q == 3 else None
                        P.op("tensor", lambda tr, g=g, c=c, bb=bb, q=q: tr.transpose(
                            carve(6 + bb, q), o_n[:, g, c * 128:(c + 1) * 128], identb_r[:]),
                             deps=trdeps if c == 0 else (), event=ev)
                    for bb in range(2):
                        oTdeps = [f"{gn}otr{bb}"]
                        if g == 0 and t >= 1:
                            oTdeps += [f"t{t-1}d3sm0", f"t{t-1}d3sm1"]
                        P.op("vector", lambda v, bb=bb: v.tensor_scalar(
                            onT[:, bb * 4:(bb + 1) * 4, :],
                            ps[:, 6 + bb, 0:256].bitcast(BF16).rearrange("p (q n) -> p q n", q=4),
                            rms_r[:], None, ALU.mult),
                             deps=oTdeps, event=f"{gn}oTe{bb}")
                    for half in range(2):
                        sbk = 2 * (g % 2) + half
                        sdeps = [f"{gn}oTe0", f"{gn}oTe1", wb_ld[half]]
                        if g >= 2:
                            sdeps.append(f"{tn}d{g-2}se")
                        elif g == 0:
                            sdeps += [f"{tn}g3cast3", f"{tn}g3cast7",
                                      f"{tn}g3oe6", f"{tn}g3oe7", f"{tn}g3sq6", f"{tn}g3sq7"]
                        for c in range(8):
                            ev = f"{gn}sm{half}" if c == 7 else None
                            P.op("tensor", lambda tr, g=g, c=c, sbk=sbk, half=half: tr.matmul(
                                bank(sbk), onT[:, c, :], wbuf[:, half, c, :],
                                start=(c == 0), stop=(c == 7)),
                                 deps=sdeps if c == 0 else (), event=ev)
    
                    sedeps = [f"{gn}sm0", f"{gn}sm1"]
                    if t >= 1:
                        sedeps.append(f"t{t-1}fsg{g}")
                    if DBG_DUMP:
                        sedeps.append(("dOUT", P.sems["dOUT"][1]))
                    P.op("vector", lambda v, g=g: v.tensor_copy(
                        s_sb[:, g, :], ps[:, 2 * (g % 2):2 * (g % 2) + 2, :].rearrange("p a n -> p (a n)")),
                         deps=sedeps, event=f"{gn}se")
                wb_ev[0] = f"{tn}d3sm0"
                wb_ev[1] = f"{tn}d3sm1"
                for fb in range(3):
                    issue_wi(t, fb)
                if t + 1 < n_tiles:
                    issue_wb(0, 0, 0)
                    issue_wb(0, 1, 1)

            with phase_blk():
                for fb in range(FBN):
                    en = f"{tn}e{fb}"
                    slot = fb % 3
                    pb, db = 4 + fb % 2, 6 + fb % 2
                    pdeps = [wi_ld[slot]] + xnT_done
                    if fb >= 2:
                        pdeps += [f"{tn}e{fb-2}r1", f"{tn}e{fb-2}t2"]
                    for c in range(8):
                        P.op("tensor", lambda tr, fb=fb, c=c, pb=pb, slot=slot, tp=tp: tr.matmul(
                            bank(pb), wiring[:, slot, c * 128:(c + 1) * 128], xnT[:, tp, c, :],
                            start=(c == 0), stop=False), deps=pdeps if c == 0 else ())
                    P.op("tensor", lambda tr, fb=fb, pb=pb: tr.matmul(
                        bank(pb), ball_r[:, fb * 128:(fb + 1) * 128], zm1[:],
                        start=False, stop=True), deps=[f"{tn}zm1"], event=f"{en}p1")
                    wi_fr[slot] = f"{en}p1"
                    ddeps = [f"{tn}zdf"]
                    if fb >= 2:
                        ddeps += [f"{tn}e{fb-2}pdb"]
                    P.op("tensor", lambda tr, fb=fb, db=db: tr.matmul(
                        bank(db), ball_r[:, fb * 128:(fb + 1) * 128], zdf[:],
                        start=True, stop=True), deps=ddeps, event=f"{en}pd")
                    P.op("scalar", lambda s, fb=fb, pb=pb: s.activation(
                        r1r[:, fb % 2, :], bank(pb), AF.Relu, bias=bi_r[:, fb:fb + 1]),
                         deps=[f"{en}p1"] + ([f"{tn}e{fb-2}hs"] if fb >= 2 else []),
                         event=f"{en}r1")
                    P.op("scalar", lambda s, fb=fb, db=db: s.activation(
                        scr512[:, fb % 2, :], bank(db), AF.Identity, bias=bi_r[:, fb:fb + 1]),
                         deps=[f"{en}pd"] + ([f"{tn}e{fb-2}t2"] if fb >= 2 else []),
                         event=f"{en}pdb")
                    P.op("vector", lambda v, fb=fb, pb=pb: v.tensor_tensor(
                        t2r[:, fb % 2, :], bank(pb), scr512[:, fb % 2, :], ALU.add),
                         deps=[f"{en}p1", f"{en}pdb"], event=f"{en}t2", fence=True)
    
                    hsdeps = [f"{en}r1", f"{en}t2"]
                    if DBG_DUMP and fb == 0:
                        hsdeps.append(("dOUT", P.sems["dOUT"][1]))
                    P.op("vector", lambda v, fb=fb: v.scalar_tensor_tensor(
                        hs[:, fb, :], t2r[:, fb % 2, :], 0.0, r1r[:, fb % 2, :],
                        ALU.max, ALU.add), deps=hsdeps, event=f"{en}hs")
                    if t + 1 < n_tiles and fb in (2, 5, 8, 11):
                        emit_A_group(t + 1, (fb - 2) // 3)
                    if fb + 3 < FBN:
                        issue_wi(t, fb + 3)
                for fb in range(3):
                    issue_dn(t, fb)
                for g in range(2):
                    xfdeps = [f"t{t-1}fout{g+2}"] if t >= 1 else []
                    P.op("gpsimd", lambda y, g=g, t=t: y.dma_start(
                        out=xF[:, g % 2, :], in_=xv[t, :, g, :]),
                         deps=xfdeps, dma=P.dma_sem(f"dXF{g % 2}"), event=f"{tn}xf{g}")

            with phase_blk():
                fn_ = f"{tn}f"
                if DBG_DUMP:
                    for g in range(G):
                        P.op("sync", lambda y, g=g, t=t: y.dma_start(
                            out=dsall[t, :, g, :], in_=s_sb[:, g, :]),
                             deps=[f"{tn}d{g}se"], dma="dOUT")
                    P.op("sync", lambda y, t=t: y.dma_start(
                        out=dhall[t].rearrange("p f n -> p (f n)"),
                        in_=hs[:].rearrange("p f n -> p (f n)")),
                         deps=[f"{tn}e{FBN-1}hs", f"{tn}e{FBN-2}hs"], dma="dOUT")
                    P.op("sync", lambda y, t=t: y.dma_start(
                        out=dvall[t], in_=vsum[:]),
                         deps=[f"{tn}vs{g_}" for g_ in range(G)], dma="dOUT")
                idep = [f"{tn}e{FBN-2}r1", f"{tn}e{FBN-2}t2", f"{tn}e{FBN-1}r1", f"{tn}e{FBN-1}t2",
                        f"{tn}e{FBN-2}pdb", f"{tn}e{FBN-1}pdb", f"{tn}d2se", f"{tn}d3se"]
                for g in range(G):
                    for half in range(2):
                        yb = 2 * g + half
                        P.op("tensor", lambda tr, yb=yb, half=half: tr.matmul(
                            bank(yb), ones_r[0:1, 0:128], bo2_row[0:1, half * 512:(half + 1) * 512],
                            start=True, stop=False, skip_group_check=True),
                             deps=idep if yb == 0 else ())
                for fb in range(FBN):
                    slot = fb % 3
                    for g in range(G):
                        for half in range(2):
                            yb = 2 * g + half
                            last = fb == FBN - 1
                            ev = f"{fn_}fb{fb}" if (g == G - 1 and half == 1) else None
                            P.op("tensor", lambda tr, fb=fb, g=g, half=half, yb=yb, slot=slot, last=last: tr.matmul(
                                bank(yb), hs[:, fb, g * 128:(g + 1) * 128],
                                dnring[:, slot, half * 512:(half + 1) * 512],
                                start=False, stop=last, skip_group_check=True),
                                 deps=[dn_ld[slot], f"{tn}e{fb}hs"] if (g == 0 and half == 0) else (),
                                 event=ev)
                    dn_fr[slot] = f"{fn_}fb{fb}"
                    if fb + 3 < FBN:
                        issue_dn(t, fb + 3)
                for g in range(G):
                    sideps = [f"{fn_}fb{FBN-1}", f"{tn}vs{g}"]
                    if g >= 1:
                        sideps.append(f"{fn_}sg{g-1}")
                    elif t >= 1:
                        sideps.append(f"t{t-1}fsg3")
                    if DBG_DUMP and g >= 1:
                        sideps.append(("dOUT", P.sems["dOUT"][1]))
                    P.op("scalar", lambda s, g=g: s.activation(
                        g_sb[:], ps[:, 2 * g:2 * g + 2, :].rearrange("p a n -> p (a n)"),
                        AF.Silu, scale=vsum[:, g:g + 1]),
                         deps=sideps, event=f"{fn_}si{g}")
    
                    sgdeps = [f"{fn_}si{g}", f"{tn}d{g}se"]
                    if last_store[g % 2]:
                        sgdeps.append(last_store[g % 2])
                    P.op("vector", lambda v, g=g: v.tensor_tensor(
                        outr[:, g % 2, :], s_sb[:, g, :], g_sb[:], ALU.mult),
                         deps=sgdeps, event=f"{fn_}sg{g}", fence=True)
                    odeps = [f"{tn}xf{g}"]
                    P.op("vector", lambda v, g=g: v.tensor_tensor(
                        outr[:, g % 2, :], outr[:, g % 2, :], xF[:, g % 2, :], ALU.add),
                         deps=odeps, event=f"{fn_}out{g}")
                    if DBG_DUMP:
                        P.op("sync", lambda y, g=g, t=t: y.dma_start(
                            out=dgall[t, :, g, :], in_=g_sb[:]),
                             deps=[f"{fn_}sg{g}"], dma="dOUT")
                    P.op("gpsimd", lambda y, g=g, t=t: y.dma_start(
                        out=ov[t, :, g, :], in_=outr[:, g % 2, :]),
                         deps=[f"{fn_}out{g}"], dma=P.dma_sem(f"dO{g % 2}"), event=f"{fn_}st{g}")
                    last_store[g % 2] = f"{fn_}st{g}"
                    if g < 2:
                        P.op("gpsimd", lambda y, g=g, t=t: y.dma_start(
                            out=xF[:, g % 2, :], in_=xv[t, :, g + 2, :]),
                             deps=[f"{fn_}out{g}"], dma=P.dma_sem(f"dXF{g % 2}"),
                             event=f"{tn}xf{g+2}")
                    if False and t + 1 < n_tiles:
                        P.op("sync", lambda y, g=g, t=t: y.dma_start(
                            out=x_sb[:, g, :], in_=xv[t + 1, :, g, :]),
                             deps=[f"{fn_}out{g}"], dma="dLD", event=f"t{t+1}xld{g}")
                        x_ld[g] = f"t{t+1}xld{g}"

        if DBG_DUMP:
            dbg_specs = [("d_xnT", xnT), ("d_qT", qT), ("d_kT", kT), ("d_v", v_sb),
                         ("d_on", o_n), ("d_s", s_sb), ("d_zm1", zm1), ("d_zdf", zdf),
                         ("d_zT", zT_sb), ("d_state", state), ("d_vsum", vsum),
                         ("d_eq1T", eq1T), ("d_eqdT", eqdT), ("d_g", g_sb),
                         ("d_hs", hs), ("d_ktok", ktok), ("d_onT", onT), ("d_sbf", sbf),
                         ("d_L", L_sb), ("d_gb", gb_bc), ("d_sc1", sc1), ("d_eq1b", eq1_b), ("d_eqdb", eqd_b),
                         ("d_dbgsc", dbgsc), ("d_dbge", dbge)]
            with P.block():
                for nm, buf in dbg_specs:
                    ap = buf[:]
                    shp = list(ap.shape)
                    dts = ap.tensor.dtype
                    n = 1
                    for d in shp[1:]:
                        n *= d
                    dd = nc.dram_tensor(nm, [shp[0], n], dts, kind="ExternalOutput").ap()
                    if len(shp) == 3:
                        flat_sb = ap.rearrange("p a b -> p (a b)")
                    elif len(shp) == 4:
                        flat_sb = ap.rearrange("p a b c -> p (a b c)")
                    else:
                        flat_sb = ap
                    P.op("sync", lambda y, a=dd, b=flat_sb: y.dma_start(out=a, in_=b),
                         dma="dOUT")

        if MERGE_MAIN:
            P.close_block()

        with P.block():
            fdeps = [("dOUT", P.sems["dOUT"][1])]
            for snm in ("dO0", "dO1"):
                if snm in P.sems:
                    fdeps.append((snm, P.sems[snm][1]))
            P.op("sync", lambda y: y.nop(), deps=fdeps)

        P.build()
    return nc


def _host_layout(inputs):
    a = {k: np.ascontiguousarray(np.asarray(v, dtype=np.float32)) for k, v in inputs.items()}
    tileT = lambda w: np.ascontiguousarray(w.T.reshape(8, 128, w.shape[0]).transpose(1, 0, 2))
    ins = {}
    ins["wqT"] = tileT(a["L0_attn_Wq"])
    ins["wkT"] = tileT(a["L0_attn_Wk"])
    ins["wvT"] = tileT(a["L0_attn_Wv"])
    ins["woT"] = tileT(a["L0_attn_Wo"])
    ins["wiT"] = np.ascontiguousarray(
        a["L0_ffn_wi"].reshape(32, 128, 8, 128).transpose(0, 3, 2, 1).reshape(32, 128, 1024))
    ins["dnT"] = np.ascontiguousarray(a["L0_ffn_wo"].T.reshape(32, 128, 1024))
    ins["ballT"] = np.ascontiguousarray(a["L0_ffn_lora_B"].transpose(0, 2, 1).reshape(128, 4096))
    aallT = a["L0_ffn_lora_A"].reshape(128, 1024).T
    ins["aallT"] = np.ascontiguousarray(aallT.reshape(8, 128, 128).transpose(1, 0, 2).reshape(128, 1024))
    gwT = a["L0_ffn_gate_w"].T
    ins["gwT"] = np.ascontiguousarray(gwT.reshape(8, 128, 8).transpose(1, 0, 2).reshape(128, 64))
    ins["fmq1"] = np.ascontiguousarray(a["L0_attn_fmq_w1"].T)
    ins["fmq2"] = np.ascontiguousarray(a["L0_attn_fmq_w2"].T)
    ins["fmk1"] = np.ascontiguousarray(a["L0_attn_fmk_w1"].T)
    ins["fmk2"] = np.ascontiguousarray(a["L0_attn_fmk_w2"].T)
    ins["fbq1"] = a["L0_attn_fmq_b1"].reshape(128, 1)
    ins["fbk1"] = a["L0_attn_fmk_b1"].reshape(128, 1)
    ins["fbq2"] = a["L0_attn_fmq_b2"].reshape(1, 128)
    ins["fbk2"] = a["L0_attn_fmk_b2"].reshape(1, 128)
    ins["bi"] = np.ascontiguousarray(a["L0_ffn_bi"].reshape(32, 128).T)
    ins["bo"] = a["L0_ffn_bo"].reshape(1, 1024)
    ins["gb"] = np.tile(a["L0_ffn_gate_b"].reshape(1, 8), (1, 4))
    ins["rms"] = a["L0_attn_rms_w"].reshape(128, 1)
    ins["identb"] = np.eye(128, dtype=np.float32).astype(ml_dtypes.bfloat16)
    ins["mask"] = np.triu(np.ones((128, 128), dtype=np.float32))
    ee = np.zeros((8, 128), dtype=np.float32)
    for e in range(8):
        ee[e, e * 16:(e + 1) * 16] = 1.0
    ins["eexp"] = ee.astype(ml_dtypes.bfloat16)
    ins["ones"] = np.ones((1, 512), dtype=np.float32).astype(ml_dtypes.bfloat16)
    return ins


def kernel(**inputs):
    n_tiles = DBG_TILES or (T // TILE)
    cores = DBG_CORES or list(range(8))
    shared = _host_layout(inputs)
    x = np.ascontiguousarray(np.asarray(inputs["L0_attn_hidden_states"], dtype=np.float32))
    nc = build_nc(n_tiles)
    in_maps = [dict(shared, x=x[b]) for b in range(len(cores))]
    res = run_bass_kernel_spmd(nc, in_maps, core_ids=cores)
    out = np.stack([r["out"] for r in res.results], axis=0)
    if DBG_DUMP:
        kernel.dbg = res.results
    return out


# version 25
# speedup vs baseline: 1.2298x; 1.2298x over previous
import contextlib
import numpy as np
import ml_dtypes
import concourse.bass as bass
import concourse.mybir as mybir
from concourse.bass_utils import run_bass_kernel_spmd

F32 = mybir.dt.float32
BF16 = mybir.dt.bfloat16
AF = mybir.ActivationFunctionType
ALU = mybir.AluOpType
AX = mybir.AxisListType

T, D, H = 4096, 1024, 8
TILE, G = 512, 4
FBN = 32
SCALE = 128.0 ** -0.5
EPS = 1e-5

DBG_TILES = None
DBG_CORES = None
DBG_DUMP = False
MERGE_MAIN = True


class Prog:
    ENGINES = ("sync", "tensor", "vector", "scalar", "gpsimd")

    def __init__(self, nc, es):
        self.nc = nc
        self.blocks = []
        self.cur = None
        self.events = {}
        self.sems = {}
        self._es = es
        for name in ("pe", "v", "s", "p", "dLD", "dW", "dOUT", "dPO"):
            h = es.enter_context(nc.semaphore("sem_" + name))
            self.sems[name] = [h, 0]
        self.eng_sem = {"tensor": "pe", "vector": "v", "scalar": "s", "gpsimd": "p"}

    def dma_sem(self, name):
        if name not in self.sems:
            h = self._es.enter_context(self.nc.semaphore("sem_" + name))
            self.sems[name] = [h, 0]
        return name

    def open_block(self):
        self.cur = {e: [] for e in self.ENGINES}

    def close_block(self):
        self.blocks.append(self.cur)
        self.cur = None

    @contextlib.contextmanager
    def block(self):
        self.open_block()
        yield
        self.close_block()

    _anon = [0]

    def op(self, eng, fn, deps=(), event=None, dma=None, fence=False):
        deps = list(deps)
        pend = getattr(self, "pending", None)
        if pend is None:
            self.pending = pend = {}
        if pend.get(eng):
            deps.append(pend.pop(eng))
        inc = None
        if dma is not None:
            s = self.sems[dma]
            s[1] += 16
            inc = (dma, 16)
            if event:
                self.events[event] = (dma, s[1], eng)
        elif event or fence:
            if event is None:
                self._anon[0] += 1
                event = f"_anon{self._anon[0]}"
            sname = self.eng_sem[eng]
            s = self.sems[sname]
            s[1] += 1
            inc = (sname, 1)
            self.events[event] = (sname, s[1], eng)
        if fence:
            self.pending[eng] = (self.eng_sem[eng], self.sems[self.eng_sem[eng]][1])
        self.cur[eng].append((fn, tuple(deps), inc))

    def ev(self, name):
        sname, val, _ = self.events[name]
        return (sname, val)

    def build(self):
        clear_blk = {e: [] for e in self.ENGINES}
        for name in list(self.sems):
            h = self.sems[name][0]
            clear_blk["sync"].append((lambda y, h=h: y.sem_clear(h), (), None))
        self.blocks.insert(0, clear_blk)
        waited = {}
        for blk in self.blocks:
            with self.nc.Block() as block:
                for ename in self.ENGINES:
                    ops = blk[ename]
                    if not ops:
                        continue

                    def body(engine, ops=ops, ename=ename):
                        for fn, deps, inc in ops:
                            need = {}
                            for dep in deps:
                                if isinstance(dep, str):
                                    sname, val, producer = self.events[dep]
                                else:
                                    sname, val = dep
                                if need.get(sname, 0) < val:
                                    need[sname] = val
                            for sname, val in need.items():
                                key = (ename, sname)
                                if waited.get(key, 0) < val:
                                    engine.wait_ge(self.sems[sname][0], val)
                                    waited[key] = val
                            ins = fn(engine)
                            if inc is not None:
                                ins.then_inc(self.sems[inc[0]][0], inc[1])

                    getattr(block, ename)(body)


def build_nc(n_tiles):
    nc = bass.Bass("TRN2", target_bir_lowering=False)
    din = lambda name, shape, dt=F32: nc.dram_tensor(name, shape, dt, kind="ExternalInput").ap()
    dsc = lambda name, shape, dt=BF16: nc.dram_tensor(name, shape, dt, kind="Internal").ap()

    x_d = din("x", [T, D])
    wT_d = [din(n, [128, 8, 1024]) for n in ("wqT", "wkT", "wvT", "woT")]
    wiT_d = din("wiT", [FBN, 128, 1024])
    dnT_d = din("dnT", [FBN, 128, 1024])
    ballT_d = din("ballT", [128, 4096])
    aallT_d = din("aallT", [128, 1024])
    gwT_d = din("gwT", [128, 64])
    fm_d = [din(n, [128, 128]) for n in ("fmq1", "fmq2", "fmk1", "fmk2")]
    fbq1_d = din("fbq1", [128, 1]); fbk1_d = din("fbk1", [128, 1])
    fbq2_d = din("fbq2", [1, 128]); fbk2_d = din("fbk2", [1, 128])
    bi_d = din("bi", [128, 32])
    bo_d = din("bo", [1, 1024])
    gb_d = din("gb", [1, 32])
    rms_d = din("rms", [128, 1])
    identb_d = din("identb", [128, 128], BF16)
    mask_d = din("mask", [128, 128])
    eexp_d = din("eexp", [8, 128], BF16)
    ones_d = din("ones", [1, 512], BF16)
    out_d = nc.dram_tensor("out", [T, D], F32, kind="ExternalOutput").ap()

    wT_s = [dsc(n, [128, 8, 1024]) for n in ("wqT_s", "wkT_s", "wvT_s", "woT_s")]
    wiT_s = dsc("wiT_s", [FBN, 128, 1024])
    dnT_s = dsc("dnT_s", [FBN, 128, 1024])

    es = contextlib.ExitStack()
    sb = lambda name, shape, dt: es.enter_context(nc.sbuf_tensor(name, shape, dt))

    ball_r = sb("ball_r", [128, 4096], BF16)
    aall_r = sb("aall_r", [128, 1024], BF16)
    gw_r = sb("gw_r", [128, 64], BF16)
    fm_r = [sb(n, [128, 128], BF16) for n in ("fmq1_r", "fmq2_r", "fmk1_r", "fmk2_r")]
    fbq1_r = sb("fbq1_r", [128, 1], F32)
    fbk1_r = sb("fbk1_r", [128, 1], F32)
    fbq2_row = sb("fbq2_row", [1, 128], BF16)
    fbk2_row = sb("fbk2_row", [1, 128], BF16)
    gb_row = sb("gb_row", [1, 32], BF16)
    bo2_row = sb("bo2_row", [1, 1024], BF16)
    bi_r = sb("bi_r", [128, 32], F32)
    gb_bc = sb("gb_bc", [128, 32], F32)
    rms_r = sb("rms_r", [128, 1], F32)
    eps_c = sb("eps_c", [128, 1], F32)
    identb_r = sb("identb_r", [128, 128], BF16)
    mask_r = sb("mask_r", [128, 128], F32)
    eexp_r = sb("eexp_r", [8, 128], BF16)
    ones_r = sb("ones_r", [1, 512], BF16)
    x_sb = sb("x_sb", [128, G, 1024], F32)
    xng = sb("xng", [128, 4, 1024], BF16)
    xnT = sb("xnT", [128, 2, 8, 512], BF16)
    xF = sb("xF", [128, 2, 1024], F32)

    ps = es.enter_context(nc.psum_tensor("ps", [128, 8, 512], F32))
    bank = lambda b: ps[:, b, :]

    def carve(b, idx):
        return ps[:, b, idx * 64:(idx + 1) * 64].bitcast(BF16)

    def carve8(b, idx):
        return ps[0:8, b, idx * 64:(idx + 1) * 64].bitcast(BF16)

    P = Prog(nc, es)

    with es:
        xv = x_d.rearrange("(t g p) d -> t p g d", p=128, g=G)
        ov = out_d.rearrange("(t g p) d -> t p g d", p=128, g=G)

        with P.block():
            loads = [(identb_r[:], identb_d), (mask_r[:], mask_d),
                     (eexp_r[:], eexp_d), (ones_r[:], ones_d), (fbq1_r[:], fbq1_d),
                     (fbk1_r[:], fbk1_d), (bi_r[:], bi_d), (rms_r[:], rms_d),
                     (x_sb[0:1, 0, 0:1024], bo_d), (x_sb[:, 1, 0:128], fm_d[0])]
            for i, (dst, src_) in enumerate(loads):
                P.op("sync", lambda y, d=dst, s=src_: y.dma_start(out=d, in_=s),
                     dma="dLD", event=f"p0ld{i}")
            allld = [f"p0ld{i}" for i in range(len(loads))]
            for nm, dst, src_ in [("cfm1", fm_r[1][:], fm_d[1]), ("cfm2", fm_r[2][:], fm_d[2]),
                                  ("cfm3", fm_r[3][:], fm_d[3]),
                                  ("cfq2", fbq2_row[:], fbq2_d), ("cfk2", fbk2_row[:], fbk2_d),
                                  ("cgb", gb_row[:], gb_d),
                                  ("caall", aall_r[:], aallT_d), ("cgw", gw_r[:], gwT_d),
                                  ("cball", ball_r[:], ballT_d)]:
                P.op("gpsimd", lambda y, d=dst, s=src_: y.dma_start(out=d, in_=s),
                     dma=P.dma_sem("d" + nm), event=nm)
            smalls = ["cfm1", "cfm2", "cfm3", "cfq2", "cfk2", "cgb", "caall", "cgw"]
            P.op("vector", lambda v: v.memset(eps_c[:], EPS), deps=allld + smalls)
            P.op("vector", lambda v: v.tensor_scalar(fm_r[0][:], x_sb[:, 1, 0:128], SCALE, None, ALU.mult))
            P.op("vector", lambda v: v.tensor_scalar(fbq1_r[:], fbq1_r[:], SCALE, None, ALU.mult))
            P.op("vector", lambda v: v.tensor_scalar(bo2_row[:], x_sb[0:1, 0, 0:1024], 2.0, None, ALU.mult))
            P.op("vector", lambda v: v.memset(state[:], 0.0), event="p0v")
            for g in range(G):
                P.op("gpsimd", lambda y, g=g: y.dma_start(out=x_sb[:, g, :], in_=xv[0, :, g, :]),
                     deps=["p0v"], dma=P.dma_sem(f"dX{g}"), event=f"t0xld{g}")
            for nm, dst, src_ in [("cwq", wT_s[0], wT_d[0]), ("cwk", wT_s[1], wT_d[1]),
                                  ("cwv", wT_s[2], wT_d[2]), ("cwo", wT_s[3], wT_d[3]),
                                  ("cwi", wiT_s.rearrange("f p j -> (f p) j"),
                                   wiT_d.rearrange("f p j -> (f p) j")),
                                  ("cdn", dnT_s.rearrange("f p j -> (f p) j"),
                                   dnT_d.rearrange("f p j -> (f p) j"))]:
                P.op("gpsimd", lambda y, d=dst, s=src_: y.dma_start(out=d, in_=s),
                     dma=P.dma_sem("d" + nm), event=nm)
            P.op("tensor", lambda t: t.matmul(ps[:, 0, 0:32], ones_r[0:1, 0:128], gb_row[:],
                                              start=True, stop=True), deps=["p0v", "cgb"], event="p0mm")
            P.op("scalar", lambda s: s.copy(gb_bc[:], ps[:, 0, 0:32]), deps=["p0mm"], event="p0gb")


        wbuf = sb("wbuf", [128, 2, 8, 512], BF16)
        q0T = sb("q0T", [128, 2, 512], BF16)
        scr512 = sb("scr512", [128, 2, 512], BF16)
        qT = sb("qT", [128, 8, 512], BF16)
        kT = sb("kT", [128, 8, 512], BF16)
        ktok = sb("ktok", [128, 8, 128], BF16)
        v_sb = sb("v_sb", [128, G, 1024], BF16)
        sbf = sb("sbf", [128, 8, 128], BF16)
        state = sb("state", [128, 1024], F32)
        atm = sb("atm", [128, 2, 128], BF16)
        o_n = sb("o_n", [128, G, 1024], BF16)
        onT = sb("onT", [128, 8, 128], BF16)
        s_sb = sb("s_sb", [128, G, 1024], BF16)
        zT_sb = sb("zT_sb", [128, 512], BF16)
        zm1 = sb("zm1", [128, 512], BF16)
        zdf = sb("zdf", [128, 512], BF16)
        L_sb = sb("L_sb", [128, 4, 8], F32)
        L2_sb = sb("L2_sb", [128, 8], F32)
        eq1_b = sb("eq1_b", [128, 8], BF16)
        eq1g = sb("eq1g", [128, 4, 8], BF16)
        eqdg = sb("eqdg", [128, 4, 8], BF16)
        eq2_b = sb("eq2_b", [128, 8], BF16)
        eqd_b = sb("eqd_b", [128, 8], BF16)
        eq1T = sb("eq1T", [8, 512], BF16)
        eqdT = sb("eqdT", [8, 512], BF16)
        vsum = sb("vsum", [128, G], F32)
        sc1 = sb("sc1", [128, 16], F32)
        rst8 = sb("rst8", [128, 8], F32)
        hs = sb("hs", [128, FBN, 512], BF16)
        wiring = sb("wiring", [128, 3, 1024], BF16)
        dnring = sb("dnring", [128, 3, 1024], BF16)
        r1r = sb("r1r", [128, 2, 512], BF16)
        t2r = sb("t2r", [128, 2, 512], BF16)
        g_sb = sb("g_sb", [128, 1024], F32)
        outr = sb("outr", [128, 2, 1024], F32)
        junk = sb("junk", [128, 1024], BF16)
        dbgsc = sb("dbgsc", [128, 64], F32)
        dbge = sb("dbge", [128, 32], F32)

        if DBG_DUMP:
            dsall = nc.dram_tensor("d_s_all", [T // TILE, 128, G, 1024], BF16,
                                   kind="ExternalOutput").ap()
            dgall = nc.dram_tensor("d_g_all", [T // TILE, 128, G, 1024], F32,
                                   kind="ExternalOutput").ap()
            dhall = nc.dram_tensor("d_hs_all", [T // TILE, 128, FBN, 512], BF16,
                                   kind="ExternalOutput").ap()
            dvall = nc.dram_tensor("d_vs_all", [T // TILE, 128, G], F32,
                                   kind="ExternalOutput").ap()

        wb_ev = [None, None]
        wb_ld = [None, None]
        wi_ld = [None] * 3
        wi_fr = [None] * 3
        dn_ld = [None] * 3
        dn_fr = [None] * 3
        x_ld = [f"t0xld{g}" for g in range(G)]
        last_store = [None, None]
        nwb = [0]

        def issue_wb(widx, half, slot, deps=()):
            nwb[0] += 1
            name = f"wb{nwb[0]}"
            d = list(deps) + [["cwq", "cwk", "cwv", "cwo"][widx]]
            if wb_ev[slot]:
                d.append(wb_ev[slot])
            P.op("sync", lambda y, w=widx, h=half, sl=slot: y.dma_start(
                out=wbuf[:, sl, :, :], in_=wT_s[w][:, :, h * 512:(h + 1) * 512]),
                 deps=d, dma=P.dma_sem(f"dWB{slot}"), event=name)
            wb_ld[slot] = name

        def issue_wi(t, fb):
            slot = fb % 3
            name = f"wi{t}_{fb}"
            d = [wi_fr[slot]] if wi_fr[slot] else ["cwi"]
            P.op("sync", lambda y, fb=fb, sl=slot: y.dma_start(
                out=wiring[:, sl, :], in_=wiT_s[fb]), deps=d, dma=P.dma_sem(f"dWI{slot}"), event=name)
            wi_ld[slot] = name

        def issue_dn(t, fb):
            slot = fb % 3
            name = f"dn{t}_{fb}"
            d = [dn_fr[slot]] if dn_fr[slot] else ["cdn"]
            P.op("sync", lambda y, fb=fb, sl=slot: y.dma_start(
                out=dnring[:, sl, :], in_=dnT_s[fb]), deps=d, dma=P.dma_sem(f"dDN{slot}"), event=name)
            dn_ld[slot] = name


        with P.block():
            P.op("vector", lambda v: v.memset(state[:], 0.0), event="st0")

        phase_blk = contextlib.nullcontext if MERGE_MAIN else P.block
        if MERGE_MAIN:
            P.open_block()
        def emit_A_dma(t):
            with phase_blk():
                for g in range(G):
                    if t > 0:
                        P.op("gpsimd", lambda y, g=g, t=t: y.dma_start(
                            out=x_sb[:, g, :], in_=xv[t, :, g, :]),
                             deps=[f"t{t-1}xn{g}", f"t{t-1}rstd{g}"], dma=P.dma_sem(f"dX{g}"),
                             event=f"t{t}xld{g}")
                    x_ld[g] = f"t{t}xld{g}"

        def emit_A(t):
            with phase_blk():
                for g in range(G):
                    emit_A_group(t, g)

        def emit_A_group(t, g):
            tn = f"t{t}"
            if True:
                if True:
                    gd = [x_ld[g]]
                    P.op("vector", lambda v, g=g: v.reduce_sum(sc1[:, g:g + 1], x_sb[:, g, :], axis=AX.X), deps=gd, fence=True)
                    P.op("vector", lambda v, g=g: v.tensor_scalar(
                        sc1[:, g:g + 1], sc1[:, g:g + 1], -1.0 / D, None, ALU.mult), event=f"{tn}nm{g}")
                    P.op("scalar", lambda s, g=g: s.activation(
                        junk[:], x_sb[:, g, :], AF.Square, bias=sc1[:, g:g + 1],
                        accum_out=sc1[:, 8 + g:9 + g]), deps=[f"{tn}nm{g}"] + gd)
                    P.op("scalar", lambda s, g=g: s.activation(
                        sc1[:, 8 + g:9 + g], sc1[:, 8 + g:9 + g], AF.Ln, scale=1.0 / D, bias=eps_c[:]))
                    P.op("scalar", lambda s, g=g: s.activation(
                        sc1[:, 8 + g:9 + g], sc1[:, 8 + g:9 + g], AF.Exp, scale=-0.5), event=f"{tn}rstd{g}")
                    xdeps = [f"{tn}rstd{g}"]
                    if t >= 1:
                        xdeps.append(f"t{t-1}xt{g}_pe")
                    P.op("vector", lambda v, g=g: v.tensor_scalar(
                        xng[:, g, :], x_sb[:, g, :], sc1[:, g:g + 1], sc1[:, 8 + g:9 + g],
                        ALU.add, ALU.mult), deps=xdeps, event=f"{tn}xn{g}")

        def emit_A_tr(t):
            tn = f"t{t}"
            tp = t % 2
            with phase_blk():
                for g in range(G):
                    trdeps = [f"{tn}xn{g}"]
                    if g >= 1:
                        trdeps += [f"{tn}xe{g-1}_0", f"{tn}xe{g-1}_1"]
                    elif t >= 1:
                        trdeps.append(f"t{t-1}fsi0")
                    else:
                        trdeps.append("p0gb")
                    for c in range(8):
                        b, q = c // 4, c % 4
                        ev = f"{tn}xt{g}_pe" if c == 7 else None
                        P.op("tensor", lambda tr, g=g, c=c, b=b, q=q: tr.transpose(
                            carve(b, q), xng[:, g, c * 128:(c + 1) * 128],
                            identb_r[:]), deps=trdeps if c == 0 else (), event=ev)
                    for b in range(2):
                        xedeps = [f"{tn}xt{g}_pe"]
                        P.op("vector", lambda v, g=g, b=b, tp=tp: v.tensor_copy(
                            xnT[:, tp, b * 4:(b + 1) * 4, g * 128:(g + 1) * 128],
                            ps[:, b, 0:256].bitcast(BF16).rearrange("p (q n) -> p q n", q=4)),
                             deps=xedeps, event=f"{tn}xe{g}_{b}")

        for t in range(n_tiles):
            tn = f"t{t}"
            tp = t % 2
            xnT_done = [f"{tn}xe{g}_{b}" for g in range(G) for b in range(2)]
            if t == 0:
                emit_A(0)
                issue_wb(0, 0, 0)
                issue_wb(0, 1, 1)
            emit_A_tr(t)
            if t + 1 < n_tiles:
                emit_A_dma(t + 1)

            with phase_blk():
                for wi_, dstT, w1, w2, b1c, b2row in [
                        (0, qT, fm_r[0], fm_r[1], fbq1_r, fbq2_row),
                        (1, kT, fm_r[2], fm_r[3], fbk1_r, fbk2_row)]:
                    wn = f"{tn}w{wi_}"
                    pwn = f"{tn}w0"

                    def prev(h, suffix):
                        if h >= 2:
                            return [f"{wn}{suffix}{h-2}"]
                        if wi_ == 1:
                            return [f"{pwn}{suffix}{h+6}"]
                        if t >= 1:
                            return [f"t{t-1}w1{suffix}{h+6}"]
                        return []

                    for h in range(H):
                        half = h // 4
                        qb = h % 2
                        deps = [wb_ld[half]] + xnT_done + prev(h, "qe")
                        if wi_ == 0 and h < 2:
                            deps += [f"{tn}xe3_0", f"{tn}xe3_1"]
                        for c in range(8):
                            ev = f"{wn}q{h}" if c == 7 else None
                            P.op("tensor", lambda tr, h=h, c=c, qb=qb, half=half, tp=tp: tr.matmul(
                                bank(qb), wbuf[:, half, c, (h % 4) * 128:(h % 4 + 1) * 128],
                                xnT[:, tp, c, :], start=(c == 0), stop=(c == 7)),
                                 deps=deps if c == 0 else (), event=ev)
                        qed = [f"{wn}q{h}"] + prev(h, "fm")
                        P.op("vector", lambda v, h=h, qb=qb: v.tensor_copy(
                            q0T[:, h % 2, :], bank(qb)), deps=qed, event=f"{wn}qe{h}")
                        f1, f2 = 2 + h % 2, 4 + h % 2
                        d1 = [f"{wn}qe{h}"] + prev(h, "t1_")
                        if wi_ == 0 and h < 2 and t >= 1:
                            d1.append(f"t{t-1}fsi1")
                        P.op("tensor", lambda tr, h=h, f1=f1, w1=w1: tr.matmul(
                            bank(f1), w1[:], q0T[:, h % 2, :], start=True, stop=True), deps=d1)
                        d2 = [f"{wn}qe{h}"] + prev(h, "qT")
                        if wi_ == 0 and h < 2 and t >= 1:
                            d2.append(f"t{t-1}fsi2")
                        P.op("tensor", lambda tr, h=h, f2=f2, w2=w2: tr.matmul(
                            bank(f2), w2[:], q0T[:, h % 2, :], start=True, stop=False), deps=d2)
                        P.op("tensor", lambda tr, h=h, f2=f2, b2row=b2row: tr.matmul(
                            bank(f2), b2row[:], ones_r[:], start=False, stop=True),
                             event=f"{wn}fm{h}")
                        t1deps = [f"{wn}fm{h}"] + prev(h, "qT")
                        if wi_ == 0 and h < 2 and t >= 1:
                            t1deps.append(f"t{t-1}e{30 + h % 2}t2")
                        P.op("scalar", lambda s, h=h, f1=f1, b1c=b1c: s.activation(
                            scr512[:, h % 2, :], bank(f1), AF.Identity, bias=b1c[:]),
                             deps=t1deps, event=f"{wn}t1_{h}")
                        qTdeps = [f"{wn}t1_{h}"]
                        if t >= 1:
                            qTdeps.append(f"t{t-1}g3o{h}")
                            if wi_ == 1:
                                qTdeps.append(f"t{t-1}g3ktr{h // 4}")
                        P.op("vector", lambda v, h=h, f2=f2, dstT=dstT: v.tensor_tensor(
                            dstT[:, h, :], bank(f2), scr512[:, h % 2, :], ALU.mult),
                             deps=qTdeps, event=f"{wn}qT{h}")
                    wb_ev[0] = f"{wn}q3"
                    wb_ev[1] = f"{wn}q7"
                    if wi_ == 0:
                        issue_wb(1, 0, 0)
                        issue_wb(1, 1, 1)
                    else:
                        issue_wb(2, 0, 0)
                        issue_wb(2, 1, 1)

                vfree = [f"{tn}w1qe6", f"{tn}w1qe7", f"{tn}w1t1_6", f"{tn}w1t1_7"]
                for half in range(2):
                    for g in range(G):
                        vb = 2 * (g % 2) + half
                        deps = [wb_ld[half]] + vfree
                        if g >= 2:
                            deps.append(f"{tn}ve{g-2}_{half}")
                        for c in range(8):
                            ev = f"{tn}vm{g}_{half}" if c == 7 else None
                            P.op("tensor", lambda tr, g=g, c=c, vb=vb, half=half, tp=tp: tr.matmul(
                                bank(vb), xnT[:, tp, c, g * 128:(g + 1) * 128],
                                wbuf[:, half, c, :], start=(c == 0), stop=(c == 7)),
                                 deps=deps if c == 0 else (), event=ev)
                        vedeps = [f"{tn}vm{g}_{half}"]
                        if t >= 1:
                            vedeps += [f"t{t-1}g{g}kv0", f"t{t-1}g{g}kv1"]
                        P.op("vector", lambda v, g=g, half=half, vb=vb: v.tensor_copy(
                            v_sb[:, g, half * 512:(half + 1) * 512], bank(vb)),
                             deps=vedeps, event=f"{tn}ve{g}_{half}")
                wb_ev[0] = f"{tn}vm3_0"
                wb_ev[1] = f"{tn}vm3_1"
                issue_wb(3, 0, 0)
                issue_wb(3, 1, 1)

                zdeps = xnT_done + [f"{tn}w1qT6", f"{tn}w1qT7"]
                if t >= 1:
                    zdeps.append(f"t{t-1}fsi3")
                for c in range(8):
                    P.op("tensor", lambda tr, c=c, tp=tp: tr.matmul(
                        bank(6), aall_r[:, c * 128:(c + 1) * 128], xnT[:, tp, c, :],
                        start=(c == 0), stop=(c == 7)),
                         deps=zdeps if c == 0 else (), event=f"{tn}zmm" if c == 7 else None)
                zedeps = [f"{tn}zmm"]
                if t >= 1:
                    zedeps += [f"t{t-1}e{FBN-1}p1", f"t{t-1}e{FBN-1}pd"]
                P.op("vector", lambda v: v.tensor_copy(zT_sb[:], bank(6)), deps=zedeps,
                     event=f"{tn}ze")

                lg0deps = xnT_done + ([f"t{t-1}fsi3", f"t{t-1}m1mm", f"t{t-1}mdmm"] if t >= 1 else [])
                for g in range(G):
                    for c in range(8):
                        ev = f"{tn}lgall" if (g == G - 1 and c == 7) else None
                        P.op("tensor", lambda tr, g=g, c=c, tp=tp: tr.matmul(
                            ps[:, 7, g * 8:(g + 1) * 8], xnT[:, tp, c, g * 128:(g + 1) * 128],
                            gw_r[:, c * 8:(c + 1) * 8], start=(c == 0), stop=(c == 7)),
                             deps=lg0deps if (g == 0 and c == 0) else (), event=ev)
                Ldeps = [f"{tn}lgall", "p0gb"]
                if t >= 1:
                    Ldeps.append(f"t{t-1}gex3")
                P.op("vector", lambda v: v.tensor_tensor(
                    L_sb[:].rearrange("p a e -> p (a e)"), ps[:, 7, 0:32], gb_bc[:], ALU.add),
                     deps=Ldeps, event=f"{tn}Leall", fence=True)
            with phase_blk():
                for g in range(G):
                    cg = t * G + g
                    gn = f"{tn}g{g}"
                    pgn = f"{tn}g{g-1}" if g >= 1 else f"t{t-1}g{G-1}"
                    for bb in range(2):
                        tdeps = []
                        if cg >= 1:
                            tdeps.append(f"{pgn}tke{bb}")
                        if g == 0:
                            tdeps += [f"{tn}ze", f"{tn}Leall"]
                        for hh in range(4):
                            h = bb * 4 + hh
                            ev = f"{gn}ktr{bb}" if hh == 3 else None
                            P.op("tensor", lambda tr, h=h, bb=bb, hh=hh, g=g: tr.transpose(
                                carve(6 + bb, hh), kT[:, h, g * 128:(g + 1) * 128], identb_r[:]),
                                 deps=(tdeps + [f"{tn}w1qT{h}"]) if hh == 0 else (), event=ev)
                        P.op("vector", lambda v, bb=bb, g=g: v.tensor_copy(
                            ktok[:, bb * 4:(bb + 1) * 4, :],
                            ps[:, 6 + bb, 0:256].bitcast(BF16).rearrange("p (q n) -> p q n", q=4)),
                             deps=[f"{gn}ktr{bb}"], event=f"{gn}tke{bb}")
                    for h in range(H):
                        ob, ab = 2 + h % 2, 4 + h % 2
                        hp = h - 2 if h >= 2 else h + 6
                        pn = gn if h >= 2 else pgn
                        adeps = [f"{tn}w1qT{h}", f"{tn}w0qT{h}"]
                        if cg >= 1 or h >= 2:
                            adeps.append(f"{pn}atm{hp}")
                        if g == 0 and h < 2:
                            adeps += [f"{tn}w1qT6", f"{tn}w1qT7"]
                        P.op("tensor", lambda tr, h=h, g=g, ab=ab: tr.matmul(
                            bank(ab)[:, 0:128], kT[:, h, g * 128:(g + 1) * 128],
                            qT[:, h, g * 128:(g + 1) * 128], start=True, stop=True),
                             deps=adeps, event=f"{gn}A{h}")
                        amdeps = [f"{gn}A{h}"]
                        if cg >= 1 or h >= 2:
                            amdeps.append(f"{pn}o{hp}")
                        P.op("vector", lambda v, h=h, ab=ab: v.tensor_tensor(
                            atm[:, h % 2, :], bank(ab)[:, 0:128], mask_r[:], ALU.mult),
                             deps=amdeps, event=f"{gn}atm{h}")
                        odeps = []
                        if cg >= 1 or h >= 2:
                            odeps += [f"{pn}oe{hp}", f"{pn}sq{hp}"]
                        if g == 0 and h < 2:
                            odeps += [f"{tn}ve3_0", f"{tn}ve3_1"]
                        if cg > 0:
                            P.op("tensor", lambda tr, h=h, g=g, ob=ob: tr.matmul(
                                bank(ob)[:, 0:128], qT[:, h, g * 128:(g + 1) * 128],
                                sbf[:, h, :], start=True, stop=False),
                                 deps=odeps + [f"{pgn}cast{h}"])
                            P.op("tensor", lambda tr, h=h, g=g, ob=ob: tr.matmul(
                                bank(ob)[:, 0:128], atm[:, h % 2, :], v_sb[:, g, h * 128:(h + 1) * 128],
                                start=False, stop=True),
                                 deps=[f"{gn}atm{h}", f"{tn}ve{g}_{h // 4}"], event=f"{gn}o{h}")
                        else:
                            P.op("tensor", lambda tr, h=h, g=g, ob=ob: tr.matmul(
                                bank(ob)[:, 0:128], atm[:, h % 2, :], v_sb[:, g, h * 128:(h + 1) * 128],
                                start=True, stop=True),
                                 deps=odeps + [f"{gn}atm{h}", f"{tn}ve{g}_{h // 4}"], event=f"{gn}o{h}")
                        sqdeps = [f"{gn}o{h}"]
                        if cg >= 1:
                            sqdeps.append(f"{pgn}oe{h}")
                        P.op("scalar", lambda s, h=h, ob=ob: s.activation(
                            junk[:, 0:128], bank(ob)[:, 0:128], AF.Square,
                            accum_out=rst8[:, h:h + 1]), deps=sqdeps)
                        P.op("scalar", lambda s, h=h: s.activation(
                            rst8[:, h:h + 1], rst8[:, h:h + 1], AF.Ln, scale=1.0 / 128, bias=eps_c[:]))
                        P.op("scalar", lambda s, h=h: s.activation(
                            rst8[:, h:h + 1], rst8[:, h:h + 1], AF.Exp, scale=-0.5),
                             event=f"{gn}sq{h}")
                    
                        oedeps = [f"{gn}sq{h}"]
                        if t >= 1:
                            oedeps.append(f"t{t-1}d{g}otr{h // 4}")
                        P.op("vector", lambda v, h=h, g=g, ob=ob: v.tensor_scalar(
                            o_n[:, g, h * 128:(h + 1) * 128], bank(ob)[:, 0:128],
                            rst8[:, h:h + 1], None, ALU.mult), deps=oedeps, event=f"{gn}oe{h}")
                    for bb in range(2):
                        kdeps = [f"{gn}tke{bb}"]
                        if cg >= 1:
                            kdeps.append(f"{pgn}cast{bb * 4 + 3}")
                        if g == 0:
                            kdeps += [f"{tn}ve2_0", f"{tn}ve2_1"]
                        for hh in range(4):
                            h = bb * 4 + hh
                            ev = f"{gn}kv{bb}" if hh == 3 else None
                            P.op("tensor", lambda tr, h=h, hh=hh, bb=bb, g=g: tr.matmul(
                                ps[:, bb, hh * 128:(hh + 1) * 128], ktok[:, h, :],
                                v_sb[:, g, h * 128:(h + 1) * 128], start=(hh == 0), stop=(hh == 3),
                                skip_group_check=True),
                                 deps=kdeps if hh == 0 else (), event=ev)
                    for bb in range(2):
                        for hh in range(4):
                            h = bb * 4 + hh
                            adddeps = [f"{gn}kv{bb}"]
                            if cg >= 1:
                                adddeps.append(f"{pgn}cast{h}")
                            if cg == 0:
                                P.op("vector", lambda v, h=h, hh=hh, bb=bb: v.tensor_copy(
                                    state[:, h * 128:(h + 1) * 128], ps[:, bb, hh * 128:(hh + 1) * 128]),
                                     deps=adddeps, event=f"{gn}add{h}")
                            else:
                                P.op("vector", lambda v, h=h, hh=hh, bb=bb: v.tensor_tensor(
                                    state[:, h * 128:(h + 1) * 128], state[:, h * 128:(h + 1) * 128],
                                    ps[:, bb, hh * 128:(hh + 1) * 128], ALU.add),
                                     deps=adddeps, event=f"{gn}add{h}")
                            P.op("scalar", lambda s, h=h: s.copy(
                                sbf[:, h, :], state[:, h * 128:(h + 1) * 128]),
                                 deps=[f"{gn}add{h}", f"{gn}o{h}"], event=f"{gn}cast{h}")
                    if True:
                        Lg = L_sb[:, g, :]
                        P.op("vector", lambda v, Lg=Lg: v.reduce_max(sc1[:, 12:13], Lg, axis=AX.X), fence=True)
                        P.op("vector", lambda v, Lg=Lg: v.tensor_scalar(eq1_b[:], Lg, sc1[:, 12:13], None, ALU.is_equal),
                             fence=True)
                        P.op("vector", lambda v, Lg=Lg: v.scalar_tensor_tensor(
                            L2_sb[:], eq1_b[:], -1e30, Lg, ALU.mult, ALU.add), fence=True)
                        P.op("vector", lambda v: v.reduce_max(sc1[:, 13:14], L2_sb[:], axis=AX.X), fence=True)
                        P.op("vector", lambda v: v.tensor_scalar(eq2_b[:], L2_sb[:], sc1[:, 13:14], None, ALU.is_equal), fence=True)
                        P.op("vector", lambda v: v.tensor_tensor(eqd_b[:], eq2_b[:], eq1_b[:], ALU.subtract),
                             event=f"{tn}eqd{g}")
                        P.op("vector", lambda v: v.tensor_scalar(sc1[:, 14:15], sc1[:, 12:13], -1.0, None, ALU.mult),
                             event=f"{tn}gnm{g}")
                        P.op("scalar", lambda s, Lg=Lg: s.activation(L2_sb[:], Lg, AF.Exp,
                                                              bias=sc1[:, 14:15]),
                             deps=[f"{tn}gnm{g}", f"{tn}eqd{g}"], event=f"{tn}gex{g}")
                        P.op("vector", lambda v: v.reduce_sum(sc1[:, 15:16], L2_sb[:], axis=AX.X),
                             deps=[f"{tn}gex{g}"], fence=True)
                        P.op("vector", lambda v: v.tensor_tensor(sc1[:, 13:14], sc1[:, 13:14],
                                                                 sc1[:, 12:13], ALU.subtract),
                             event=f"{tn}gdm{g}")
                        P.op("scalar", lambda s: s.activation(sc1[:, 12:13], sc1[:, 13:14], AF.Exp),
                             deps=[f"{tn}gdm{g}"], event=f"{tn}ge2{g}")
                        P.op("vector", lambda v: v.reciprocal(sc1[:, 15:16], sc1[:, 15:16]), fence=True)
                        vsdeps = [f"{tn}ge2{g}"]
                        if t >= 1:
                            vsdeps.append(f"t{t-1}fsi{g}")
                        if DBG_DUMP and g == 0:
                            vsdeps.append(("dOUT", P.sems["dOUT"][1]))
                        P.op("vector", lambda v, g=g: v.scalar_tensor_tensor(
                            vsum[:, g:g + 1], sc1[:, 12:13], 1.0, sc1[:, 15:16], ALU.add, ALU.mult),
                             deps=vsdeps, event=f"{tn}vs{g}")
                        eqsdeps = [f"t{t-1}et1{g}", f"t{t-1}et2{g}"] if t >= 1 else []
                        P.op("vector", lambda v, g=g: v.tensor_copy(eq1g[:, g, :], eq1_b[:]),
                             deps=eqsdeps, event=f"{tn}eqs1{g}")
                        P.op("vector", lambda v, g=g: v.tensor_copy(eqdg[:, g, :], eqd_b[:]),
                             event=f"{tn}eqs2{g}")


            with phase_blk():
                for g in range(G):
                    tdeps = [f"{tn}eqs1{g}", f"{tn}eqs2{g}", f"{tn}g3atm7"]
                    if g >= 1:
                        tdeps += [f"{tn}ee1{g-1}", f"{tn}ee2{g-1}"]
                    P.op("tensor", lambda tr, g=g: tr.transpose(carve8(5, 0), eq1g[:, g, :], identb_r[:]),
                         deps=tdeps, event=f"{tn}et1{g}")
                    P.op("tensor", lambda tr, g=g: tr.transpose(carve8(5, 1), eqdg[:, g, :], identb_r[:]),
                         event=f"{tn}et2{g}")
                    eedeps0 = [f"{tn}et1{g}"]
                    if g == 0 and t >= 1:
                        eedeps0 += [f"t{t-1}m1mm", f"t{t-1}mdmm"]
                    P.op("scalar", lambda s, g=g: s.copy(eq1T[:, g * 128:(g + 1) * 128], carve8(5, 0)),
                         deps=eedeps0, event=f"{tn}ee1{g}")
                    P.op("scalar", lambda s, g=g: s.copy(eqdT[:, g * 128:(g + 1) * 128], car8 if False else carve8(5, 1)),
                         deps=[f"{tn}et2{g}"], event=f"{tn}ee2{g}")
                eedeps = [f"{tn}ee1{g}" for g in range(G)] + [f"{tn}ee2{g}" for g in range(G)]
                P.op("tensor", lambda tr: tr.matmul(bank(4), eexp_r[:], eq1T[:], start=True, stop=True),
                     deps=eedeps + [f"{tn}ze", f"{tn}g3atm6"], event=f"{tn}m1mm")
                zm1deps = [f"{tn}m1mm"]
                if t >= 1:
                    zm1deps.append(f"t{t-1}e{FBN-1}p1")
                P.op("vector", lambda v: v.tensor_tensor(zm1[:], bank(4), zT_sb[:], ALU.mult),
                     deps=zm1deps, event=f"{tn}zm1")
                P.op("tensor", lambda tr: tr.matmul(bank(5), eexp_r[:], eqdT[:], start=True, stop=True),
                     deps=eedeps, event=f"{tn}mdmm")
                zdfdeps = [f"{tn}mdmm"]
                if t >= 1:
                    zdfdeps.append(f"t{t-1}e{FBN-1}pd")
                P.op("vector", lambda v: v.tensor_tensor(zdf[:], bank(5), zT_sb[:], ALU.mult),
                     deps=zdfdeps, event=f"{tn}zdf")


            with phase_blk():
                for g in range(G):
                    gn = f"{tn}d{g}"
                    trdeps = [f"{tn}g{g}oe{h}" for h in range(H)]
                    if g >= 1:
                        trdeps += [f"{tn}d{g-1}oTe0", f"{tn}d{g-1}oTe1"]
                    else:
                        trdeps += [f"{tn}g3tke0", f"{tn}g3tke1"]
                    for c in range(8):
                        bb, q = c // 4, c % 4
                        ev = f"{gn}otr{bb}" if q == 3 else None
                        P.op("tensor", lambda tr, g=g, c=c, bb=bb, q=q: tr.transpose(
                            carve(6 + bb, q), o_n[:, g, c * 128:(c + 1) * 128], identb_r[:]),
                             deps=trdeps if c == 0 else (), event=ev)
                    for bb in range(2):
                        oTdeps = [f"{gn}otr{bb}"]
                        if g == 0 and t >= 1:
                            oTdeps += [f"t{t-1}d3sm0", f"t{t-1}d3sm1"]
                        P.op("vector", lambda v, bb=bb: v.tensor_scalar(
                            onT[:, bb * 4:(bb + 1) * 4, :],
                            ps[:, 6 + bb, 0:256].bitcast(BF16).rearrange("p (q n) -> p q n", q=4),
                            rms_r[:], None, ALU.mult),
                             deps=oTdeps, event=f"{gn}oTe{bb}")
                    for half in range(2):
                        sbk = 2 * (g % 2) + half
                        sdeps = [f"{gn}oTe0", f"{gn}oTe1", wb_ld[half]]
                        if g >= 2:
                            sdeps.append(f"{tn}d{g-2}se")
                        elif g == 0:
                            sdeps += [f"{tn}g3cast3", f"{tn}g3cast7",
                                      f"{tn}g3oe6", f"{tn}g3oe7", f"{tn}g3sq6", f"{tn}g3sq7"]
                        for c in range(8):
                            ev = f"{gn}sm{half}" if c == 7 else None
                            P.op("tensor", lambda tr, g=g, c=c, sbk=sbk, half=half: tr.matmul(
                                bank(sbk), onT[:, c, :], wbuf[:, half, c, :],
                                start=(c == 0), stop=(c == 7)),
                                 deps=sdeps if c == 0 else (), event=ev)
    
                    sedeps = [f"{gn}sm0", f"{gn}sm1"]
                    if t >= 1:
                        sedeps.append(f"t{t-1}fsg{g}")
                    if DBG_DUMP:
                        sedeps.append(("dOUT", P.sems["dOUT"][1]))
                    P.op("vector", lambda v, g=g: v.tensor_copy(
                        s_sb[:, g, :], ps[:, 2 * (g % 2):2 * (g % 2) + 2, :].rearrange("p a n -> p (a n)")),
                         deps=sedeps, event=f"{gn}se")
                wb_ev[0] = f"{tn}d3sm0"
                wb_ev[1] = f"{tn}d3sm1"
                for fb in range(3):
                    issue_wi(t, fb)
                if t + 1 < n_tiles:
                    issue_wb(0, 0, 0)
                    issue_wb(0, 1, 1)

            with phase_blk():
                for fb in range(FBN):
                    en = f"{tn}e{fb}"
                    slot = fb % 3
                    pb, db = 4 + fb % 2, 6 + fb % 2
                    pdeps = [wi_ld[slot]] + xnT_done
                    if fb >= 2:
                        pdeps += [f"{tn}e{fb-2}r1", f"{tn}e{fb-2}t2"]
                    for c in range(8):
                        P.op("tensor", lambda tr, fb=fb, c=c, pb=pb, slot=slot, tp=tp: tr.matmul(
                            bank(pb), wiring[:, slot, c * 128:(c + 1) * 128], xnT[:, tp, c, :],
                            start=(c == 0), stop=False), deps=pdeps if c == 0 else ())
                    P.op("tensor", lambda tr, fb=fb, pb=pb: tr.matmul(
                        bank(pb), ball_r[:, fb * 128:(fb + 1) * 128], zm1[:],
                        start=False, stop=True), deps=[f"{tn}zm1"], event=f"{en}p1")
                    wi_fr[slot] = f"{en}p1"
                    ddeps = [f"{tn}zdf"]
                    if fb >= 2:
                        ddeps += [f"{tn}e{fb-2}pdb"]
                    P.op("tensor", lambda tr, fb=fb, db=db: tr.matmul(
                        bank(db), ball_r[:, fb * 128:(fb + 1) * 128], zdf[:],
                        start=True, stop=True), deps=ddeps, event=f"{en}pd")
                    P.op("scalar", lambda s, fb=fb, pb=pb: s.activation(
                        r1r[:, fb % 2, :], bank(pb), AF.Relu, bias=bi_r[:, fb:fb + 1]),
                         deps=[f"{en}p1"] + ([f"{tn}e{fb-2}hs"] if fb >= 2 else []),
                         event=f"{en}r1")
                    P.op("scalar", lambda s, fb=fb, db=db: s.activation(
                        scr512[:, fb % 2, :], bank(db), AF.Identity, bias=bi_r[:, fb:fb + 1]),
                         deps=[f"{en}pd"] + ([f"{tn}e{fb-2}t2"] if fb >= 2 else []),
                         event=f"{en}pdb")
                    P.op("vector", lambda v, fb=fb, pb=pb: v.tensor_tensor(
                        t2r[:, fb % 2, :], bank(pb), scr512[:, fb % 2, :], ALU.add),
                         deps=[f"{en}p1", f"{en}pdb"], event=f"{en}t2", fence=True)
    
                    hsdeps = [f"{en}r1", f"{en}t2"]
                    if DBG_DUMP and fb == 0:
                        hsdeps.append(("dOUT", P.sems["dOUT"][1]))
                    P.op("vector", lambda v, fb=fb: v.scalar_tensor_tensor(
                        hs[:, fb, :], t2r[:, fb % 2, :], 0.0, r1r[:, fb % 2, :],
                        ALU.max, ALU.add), deps=hsdeps, event=f"{en}hs")
                    if fb + 3 < FBN:
                        issue_wi(t, fb + 3)
                for fb in range(3):
                    issue_dn(t, fb)
                for g in range(2):
                    xfdeps = [f"t{t-1}fout{g+2}"] if t >= 1 else []
                    P.op("gpsimd", lambda y, g=g, t=t: y.dma_start(
                        out=xF[:, g % 2, :], in_=xv[t, :, g, :]),
                         deps=xfdeps, dma=P.dma_sem(f"dXF{g % 2}"), event=f"{tn}xf{g}")

            if t + 1 < n_tiles:
                emit_A(t + 1)

            with phase_blk():
                fn_ = f"{tn}f"
                if DBG_DUMP:
                    for g in range(G):
                        P.op("sync", lambda y, g=g, t=t: y.dma_start(
                            out=dsall[t, :, g, :], in_=s_sb[:, g, :]),
                             deps=[f"{tn}d{g}se"], dma="dOUT")
                    P.op("sync", lambda y, t=t: y.dma_start(
                        out=dhall[t].rearrange("p f n -> p (f n)"),
                        in_=hs[:].rearrange("p f n -> p (f n)")),
                         deps=[f"{tn}e{FBN-1}hs", f"{tn}e{FBN-2}hs"], dma="dOUT")
                    P.op("sync", lambda y, t=t: y.dma_start(
                        out=dvall[t], in_=vsum[:]),
                         deps=[f"{tn}vs{g_}" for g_ in range(G)], dma="dOUT")
                idep = [f"{tn}e{FBN-2}r1", f"{tn}e{FBN-2}t2", f"{tn}e{FBN-1}r1", f"{tn}e{FBN-1}t2",
                        f"{tn}e{FBN-2}pdb", f"{tn}e{FBN-1}pdb", f"{tn}d2se", f"{tn}d3se"]
                for g in range(G):
                    for half in range(2):
                        yb = 2 * g + half
                        P.op("tensor", lambda tr, yb=yb, half=half: tr.matmul(
                            bank(yb), ones_r[0:1, 0:128], bo2_row[0:1, half * 512:(half + 1) * 512],
                            start=True, stop=False, skip_group_check=True),
                             deps=idep if yb == 0 else ())
                for fb in range(FBN):
                    slot = fb % 3
                    for g in range(G):
                        for half in range(2):
                            yb = 2 * g + half
                            last = fb == FBN - 1
                            ev = f"{fn_}fb{fb}" if (g == G - 1 and half == 1) else None
                            P.op("tensor", lambda tr, fb=fb, g=g, half=half, yb=yb, slot=slot, last=last: tr.matmul(
                                bank(yb), hs[:, fb, g * 128:(g + 1) * 128],
                                dnring[:, slot, half * 512:(half + 1) * 512],
                                start=False, stop=last, skip_group_check=True),
                                 deps=[dn_ld[slot], f"{tn}e{fb}hs"] if (g == 0 and half == 0) else (),
                                 event=ev)
                    dn_fr[slot] = f"{fn_}fb{fb}"
                    if fb + 3 < FBN:
                        issue_dn(t, fb + 3)
                for g in range(G):
                    sideps = [f"{fn_}fb{FBN-1}", f"{tn}vs{g}"]
                    if g >= 1:
                        sideps.append(f"{fn_}sg{g-1}")
                    elif t >= 1:
                        sideps.append(f"t{t-1}fsg3")
                    if DBG_DUMP and g >= 1:
                        sideps.append(("dOUT", P.sems["dOUT"][1]))
                    P.op("scalar", lambda s, g=g: s.activation(
                        g_sb[:], ps[:, 2 * g:2 * g + 2, :].rearrange("p a n -> p (a n)"),
                        AF.Silu, scale=vsum[:, g:g + 1]),
                         deps=sideps, event=f"{fn_}si{g}")
    
                    sgdeps = [f"{fn_}si{g}", f"{tn}d{g}se"]
                    if last_store[g % 2]:
                        sgdeps.append(last_store[g % 2])
                    P.op("vector", lambda v, g=g: v.tensor_tensor(
                        outr[:, g % 2, :], s_sb[:, g, :], g_sb[:], ALU.mult),
                         deps=sgdeps, event=f"{fn_}sg{g}", fence=True)
                    odeps = [f"{tn}xf{g}"]
                    P.op("vector", lambda v, g=g: v.tensor_tensor(
                        outr[:, g % 2, :], outr[:, g % 2, :], xF[:, g % 2, :], ALU.add),
                         deps=odeps, event=f"{fn_}out{g}")
                    if DBG_DUMP:
                        P.op("sync", lambda y, g=g, t=t: y.dma_start(
                            out=dgall[t, :, g, :], in_=g_sb[:]),
                             deps=[f"{fn_}sg{g}"], dma="dOUT")
                    P.op("gpsimd", lambda y, g=g, t=t: y.dma_start(
                        out=ov[t, :, g, :], in_=outr[:, g % 2, :]),
                         deps=[f"{fn_}out{g}"], dma=P.dma_sem(f"dO{g % 2}"), event=f"{fn_}st{g}")
                    last_store[g % 2] = f"{fn_}st{g}"
                    if g < 2:
                        P.op("gpsimd", lambda y, g=g, t=t: y.dma_start(
                            out=xF[:, g % 2, :], in_=xv[t, :, g + 2, :]),
                             deps=[f"{fn_}out{g}"], dma=P.dma_sem(f"dXF{g % 2}"),
                             event=f"{tn}xf{g+2}")
                    if False and t + 1 < n_tiles:
                        P.op("sync", lambda y, g=g, t=t: y.dma_start(
                            out=x_sb[:, g, :], in_=xv[t + 1, :, g, :]),
                             deps=[f"{fn_}out{g}"], dma="dLD", event=f"t{t+1}xld{g}")
                        x_ld[g] = f"t{t+1}xld{g}"

        if DBG_DUMP:
            dbg_specs = [("d_xnT", xnT), ("d_qT", qT), ("d_kT", kT), ("d_v", v_sb),
                         ("d_on", o_n), ("d_s", s_sb), ("d_zm1", zm1), ("d_zdf", zdf),
                         ("d_zT", zT_sb), ("d_state", state), ("d_vsum", vsum),
                         ("d_eq1T", eq1T), ("d_eqdT", eqdT), ("d_g", g_sb),
                         ("d_hs", hs), ("d_ktok", ktok), ("d_onT", onT), ("d_sbf", sbf),
                         ("d_L", L_sb), ("d_gb", gb_bc), ("d_sc1", sc1), ("d_eq1b", eq1_b), ("d_eqdb", eqd_b),
                         ("d_dbgsc", dbgsc), ("d_dbge", dbge)]
            with P.block():
                for nm, buf in dbg_specs:
                    ap = buf[:]
                    shp = list(ap.shape)
                    dts = ap.tensor.dtype
                    n = 1
                    for d in shp[1:]:
                        n *= d
                    dd = nc.dram_tensor(nm, [shp[0], n], dts, kind="ExternalOutput").ap()
                    if len(shp) == 3:
                        flat_sb = ap.rearrange("p a b -> p (a b)")
                    elif len(shp) == 4:
                        flat_sb = ap.rearrange("p a b c -> p (a b c)")
                    else:
                        flat_sb = ap
                    P.op("sync", lambda y, a=dd, b=flat_sb: y.dma_start(out=a, in_=b),
                         dma="dOUT")

        if MERGE_MAIN:
            P.close_block()

        with P.block():
            fdeps = [("dOUT", P.sems["dOUT"][1])]
            for snm in ("dO0", "dO1"):
                if snm in P.sems:
                    fdeps.append((snm, P.sems[snm][1]))
            P.op("sync", lambda y: y.nop(), deps=fdeps)

        P.build()
    return nc


def _host_layout(inputs):
    a = {k: np.ascontiguousarray(np.asarray(v, dtype=np.float32)) for k, v in inputs.items()}
    tileT = lambda w: np.ascontiguousarray(w.T.reshape(8, 128, w.shape[0]).transpose(1, 0, 2))
    ins = {}
    ins["wqT"] = tileT(a["L0_attn_Wq"])
    ins["wkT"] = tileT(a["L0_attn_Wk"])
    ins["wvT"] = tileT(a["L0_attn_Wv"])
    ins["woT"] = tileT(a["L0_attn_Wo"])
    ins["wiT"] = np.ascontiguousarray(
        a["L0_ffn_wi"].reshape(32, 128, 8, 128).transpose(0, 3, 2, 1).reshape(32, 128, 1024))
    ins["dnT"] = np.ascontiguousarray(a["L0_ffn_wo"].T.reshape(32, 128, 1024))
    ins["ballT"] = np.ascontiguousarray(a["L0_ffn_lora_B"].transpose(0, 2, 1).reshape(128, 4096))
    aallT = a["L0_ffn_lora_A"].reshape(128, 1024).T
    ins["aallT"] = np.ascontiguousarray(aallT.reshape(8, 128, 128).transpose(1, 0, 2).reshape(128, 1024))
    gwT = a["L0_ffn_gate_w"].T
    ins["gwT"] = np.ascontiguousarray(gwT.reshape(8, 128, 8).transpose(1, 0, 2).reshape(128, 64))
    ins["fmq1"] = np.ascontiguousarray(a["L0_attn_fmq_w1"].T)
    ins["fmq2"] = np.ascontiguousarray(a["L0_attn_fmq_w2"].T)
    ins["fmk1"] = np.ascontiguousarray(a["L0_attn_fmk_w1"].T)
    ins["fmk2"] = np.ascontiguousarray(a["L0_attn_fmk_w2"].T)
    ins["fbq1"] = a["L0_attn_fmq_b1"].reshape(128, 1)
    ins["fbk1"] = a["L0_attn_fmk_b1"].reshape(128, 1)
    ins["fbq2"] = a["L0_attn_fmq_b2"].reshape(1, 128)
    ins["fbk2"] = a["L0_attn_fmk_b2"].reshape(1, 128)
    ins["bi"] = np.ascontiguousarray(a["L0_ffn_bi"].reshape(32, 128).T)
    ins["bo"] = a["L0_ffn_bo"].reshape(1, 1024)
    ins["gb"] = np.tile(a["L0_ffn_gate_b"].reshape(1, 8), (1, 4))
    ins["rms"] = a["L0_attn_rms_w"].reshape(128, 1)
    ins["identb"] = np.eye(128, dtype=np.float32).astype(ml_dtypes.bfloat16)
    ins["mask"] = np.triu(np.ones((128, 128), dtype=np.float32))
    ee = np.zeros((8, 128), dtype=np.float32)
    for e in range(8):
        ee[e, e * 16:(e + 1) * 16] = 1.0
    ins["eexp"] = ee.astype(ml_dtypes.bfloat16)
    ins["ones"] = np.ones((1, 512), dtype=np.float32).astype(ml_dtypes.bfloat16)
    return ins


def kernel(**inputs):
    n_tiles = DBG_TILES or (T // TILE)
    cores = DBG_CORES or list(range(8))
    shared = _host_layout(inputs)
    x = np.ascontiguousarray(np.asarray(inputs["L0_attn_hidden_states"], dtype=np.float32))
    nc = build_nc(n_tiles)
    in_maps = [dict(shared, x=x[b]) for b in range(len(cores))]
    res = run_bass_kernel_spmd(nc, in_maps, core_ids=cores)
    out = np.stack([r["out"] for r in res.results], axis=0)
    if DBG_DUMP:
        kernel.dbg = res.results
    return out


# version 26
# speedup vs baseline: 1.2461x; 1.0133x over previous
import contextlib
import numpy as np
import ml_dtypes
import concourse.bass as bass
import concourse.mybir as mybir
from concourse.bass_utils import run_bass_kernel_spmd

F32 = mybir.dt.float32
BF16 = mybir.dt.bfloat16
AF = mybir.ActivationFunctionType
ALU = mybir.AluOpType
AX = mybir.AxisListType

T, D, H = 4096, 1024, 8
TILE, G = 512, 4
FBN = 32
SCALE = 128.0 ** -0.5
EPS = 1e-5

DBG_TILES = None
DBG_CORES = None
DBG_DUMP = False
MERGE_MAIN = True


class Prog:
    ENGINES = ("sync", "tensor", "vector", "scalar", "gpsimd")

    def __init__(self, nc, es):
        self.nc = nc
        self.blocks = []
        self.cur = None
        self.events = {}
        self.sems = {}
        self._es = es
        for name in ("pe", "v", "s", "p", "dLD", "dW", "dOUT", "dPO"):
            h = es.enter_context(nc.semaphore("sem_" + name))
            self.sems[name] = [h, 0]
        self.eng_sem = {"tensor": "pe", "vector": "v", "scalar": "s", "gpsimd": "p"}

    def dma_sem(self, name):
        if name not in self.sems:
            h = self._es.enter_context(self.nc.semaphore("sem_" + name))
            self.sems[name] = [h, 0]
        return name

    def open_block(self):
        self.cur = {e: [] for e in self.ENGINES}

    def close_block(self):
        self.blocks.append(self.cur)
        self.cur = None

    @contextlib.contextmanager
    def block(self):
        self.open_block()
        yield
        self.close_block()

    _anon = [0]

    def op(self, eng, fn, deps=(), event=None, dma=None, fence=False):
        deps = list(deps)
        pend = getattr(self, "pending", None)
        if pend is None:
            self.pending = pend = {}
        if pend.get(eng):
            deps.append(pend.pop(eng))
        inc = None
        if dma is not None:
            s = self.sems[dma]
            s[1] += 16
            inc = (dma, 16)
            if event:
                self.events[event] = (dma, s[1], eng)
        elif event or fence:
            if event is None:
                self._anon[0] += 1
                event = f"_anon{self._anon[0]}"
            sname = self.eng_sem[eng]
            s = self.sems[sname]
            s[1] += 1
            inc = (sname, 1)
            self.events[event] = (sname, s[1], eng)
        if fence:
            self.pending[eng] = (self.eng_sem[eng], self.sems[self.eng_sem[eng]][1])
        self.cur[eng].append((fn, tuple(deps), inc))

    def ev(self, name):
        sname, val, _ = self.events[name]
        return (sname, val)

    def build(self):
        clear_blk = {e: [] for e in self.ENGINES}
        for name in list(self.sems):
            h = self.sems[name][0]
            clear_blk["sync"].append((lambda y, h=h: y.sem_clear(h), (), None))
        self.blocks.insert(0, clear_blk)
        waited = {}
        for blk in self.blocks:
            with self.nc.Block() as block:
                for ename in self.ENGINES:
                    ops = blk[ename]
                    if not ops:
                        continue

                    def body(engine, ops=ops, ename=ename):
                        for fn, deps, inc in ops:
                            need = {}
                            for dep in deps:
                                if isinstance(dep, str):
                                    sname, val, producer = self.events[dep]
                                else:
                                    sname, val = dep
                                if need.get(sname, 0) < val:
                                    need[sname] = val
                            for sname, val in need.items():
                                key = (ename, sname)
                                if waited.get(key, 0) < val:
                                    engine.wait_ge(self.sems[sname][0], val)
                                    waited[key] = val
                            ins = fn(engine)
                            if inc is not None:
                                ins.then_inc(self.sems[inc[0]][0], inc[1])

                    getattr(block, ename)(body)


def build_nc(n_tiles):
    nc = bass.Bass("TRN2", target_bir_lowering=False)
    din = lambda name, shape, dt=F32: nc.dram_tensor(name, shape, dt, kind="ExternalInput").ap()
    dsc = lambda name, shape, dt=BF16: nc.dram_tensor(name, shape, dt, kind="Internal").ap()

    x_d = din("x", [T, D])
    wT_d = [din(n, [128, 8, 1024]) for n in ("wqT", "wkT", "wvT", "woT")]
    wiT_d = din("wiT", [FBN, 128, 1024])
    dnT_d = din("dnT", [FBN, 128, 1024])
    ballT_d = din("ballT", [128, 4096])
    aallT_d = din("aallT", [128, 1024])
    gwT_d = din("gwT", [128, 64])
    fm_d = [din(n, [128, 128]) for n in ("fmq1", "fmq2", "fmk1", "fmk2")]
    fbq1_d = din("fbq1", [128, 1]); fbk1_d = din("fbk1", [128, 1])
    fbq2_d = din("fbq2", [1, 128]); fbk2_d = din("fbk2", [1, 128])
    bi_d = din("bi", [128, 32])
    bo_d = din("bo", [1, 1024])
    gb_d = din("gb", [1, 32])
    rms_d = din("rms", [128, 1])
    identb_d = din("identb", [128, 128], BF16)
    mask_d = din("mask", [128, 128])
    eexp_d = din("eexp", [8, 128], BF16)
    ones_d = din("ones", [1, 512], BF16)
    out_d = nc.dram_tensor("out", [T, D], F32, kind="ExternalOutput").ap()

    wT_s = [dsc(n, [128, 8, 1024]) for n in ("wqT_s", "wkT_s", "wvT_s", "woT_s")]
    wiT_s = dsc("wiT_s", [FBN, 128, 1024])
    dnT_s = dsc("dnT_s", [FBN, 128, 1024])

    es = contextlib.ExitStack()
    sb = lambda name, shape, dt: es.enter_context(nc.sbuf_tensor(name, shape, dt))

    ball_r = sb("ball_r", [128, 4096], BF16)
    aall_r = sb("aall_r", [128, 1024], BF16)
    gw_r = sb("gw_r", [128, 64], BF16)
    fm_r = [sb(n, [128, 128], BF16) for n in ("fmq1_r", "fmq2_r", "fmk1_r", "fmk2_r")]
    fbq1_r = sb("fbq1_r", [128, 1], F32)
    fbk1_r = sb("fbk1_r", [128, 1], F32)
    fbq2_row = sb("fbq2_row", [1, 128], BF16)
    fbk2_row = sb("fbk2_row", [1, 128], BF16)
    gb_row = sb("gb_row", [1, 32], BF16)
    bo2_row = sb("bo2_row", [1, 1024], BF16)
    bi_r = sb("bi_r", [128, 32], F32)
    gb_bc = sb("gb_bc", [128, 32], F32)
    rms_r = sb("rms_r", [128, 1], F32)
    eps_c = sb("eps_c", [128, 1], F32)
    identb_r = sb("identb_r", [128, 128], BF16)
    mask_r = sb("mask_r", [128, 128], F32)
    eexp_r = sb("eexp_r", [8, 128], BF16)
    ones_r = sb("ones_r", [1, 512], BF16)
    x_sb = sb("x_sb", [128, G, 1024], F32)
    xng = sb("xng", [128, 4, 1024], BF16)
    xnT = sb("xnT", [128, 2, 8, 512], BF16)
    xF = sb("xF", [128, 2, 1024], F32)

    ps = es.enter_context(nc.psum_tensor("ps", [128, 8, 512], F32))
    bank = lambda b: ps[:, b, :]

    def carve(b, idx):
        return ps[:, b, idx * 64:(idx + 1) * 64].bitcast(BF16)

    def carve8(b, idx):
        return ps[0:8, b, idx * 64:(idx + 1) * 64].bitcast(BF16)

    P = Prog(nc, es)

    with es:
        xv = x_d.rearrange("(t g p) d -> t p g d", p=128, g=G)
        ov = out_d.rearrange("(t g p) d -> t p g d", p=128, g=G)

        with P.block():
            loads = [(identb_r[:], identb_d), (mask_r[:], mask_d),
                     (eexp_r[:], eexp_d), (ones_r[:], ones_d), (fbq1_r[:], fbq1_d),
                     (fbk1_r[:], fbk1_d), (bi_r[:], bi_d), (rms_r[:], rms_d),
                     (x_sb[0:1, 0, 0:1024], bo_d), (x_sb[:, 1, 0:128], fm_d[0])]
            for i, (dst, src_) in enumerate(loads):
                P.op("sync", lambda y, d=dst, s=src_: y.dma_start(out=d, in_=s),
                     dma="dLD", event=f"p0ld{i}")
            allld = [f"p0ld{i}" for i in range(len(loads))]
            for nm, dst, src_ in [("cfm1", fm_r[1][:], fm_d[1]), ("cfm2", fm_r[2][:], fm_d[2]),
                                  ("cfm3", fm_r[3][:], fm_d[3]),
                                  ("cfq2", fbq2_row[:], fbq2_d), ("cfk2", fbk2_row[:], fbk2_d),
                                  ("cgb", gb_row[:], gb_d),
                                  ("caall", aall_r[:], aallT_d), ("cgw", gw_r[:], gwT_d),
                                  ("cball", ball_r[:], ballT_d)]:
                P.op("gpsimd", lambda y, d=dst, s=src_: y.dma_start(out=d, in_=s),
                     dma=P.dma_sem("d" + nm), event=nm)
            smalls = ["cfm1", "cfm2", "cfm3", "cfq2", "cfk2", "cgb", "caall", "cgw"]
            P.op("vector", lambda v: v.memset(eps_c[:], EPS), deps=allld + smalls)
            P.op("vector", lambda v: v.tensor_scalar(fm_r[0][:], x_sb[:, 1, 0:128], SCALE, None, ALU.mult))
            P.op("vector", lambda v: v.tensor_scalar(fbq1_r[:], fbq1_r[:], SCALE, None, ALU.mult))
            P.op("vector", lambda v: v.tensor_scalar(bo2_row[:], x_sb[0:1, 0, 0:1024], 2.0, None, ALU.mult))
            P.op("vector", lambda v: v.memset(state[:], 0.0), event="p0v")
            for g in range(G):
                P.op("gpsimd", lambda y, g=g: y.dma_start(out=x_sb[:, g, :], in_=xv[0, :, g, :]),
                     deps=["p0v"], dma=P.dma_sem(f"dX{g}"), event=f"t0xld{g}")
            for nm, dst, src_ in [("cwq", wT_s[0], wT_d[0]), ("cwk", wT_s[1], wT_d[1]),
                                  ("cwv", wT_s[2], wT_d[2]), ("cwo", wT_s[3], wT_d[3]),
                                  ("cwi", wiT_s.rearrange("f p j -> (f p) j"),
                                   wiT_d.rearrange("f p j -> (f p) j")),
                                  ("cdn", dnT_s.rearrange("f p j -> (f p) j"),
                                   dnT_d.rearrange("f p j -> (f p) j"))]:
                P.op("gpsimd", lambda y, d=dst, s=src_: y.dma_start(out=d, in_=s),
                     dma=P.dma_sem("d" + nm), event=nm)
            P.op("tensor", lambda t: t.matmul(ps[:, 0, 0:32], ones_r[0:1, 0:128], gb_row[:],
                                              start=True, stop=True), deps=["p0v", "cgb"], event="p0mm")
            P.op("scalar", lambda s: s.copy(gb_bc[:], ps[:, 0, 0:32]), deps=["p0mm"], event="p0gb")


        wbuf = sb("wbuf", [128, 2, 8, 512], BF16)
        q0T = sb("q0T", [128, 2, 512], BF16)
        scr512 = sb("scr512", [128, 2, 512], BF16)
        qT = sb("qT", [128, 8, 512], BF16)
        kT = sb("kT", [128, 8, 512], BF16)
        ktok = sb("ktok", [128, 8, 128], BF16)
        v_sb = sb("v_sb", [128, G, 1024], BF16)
        sbf = sb("sbf", [128, 8, 128], BF16)
        state = sb("state", [128, 1024], F32)
        atm = sb("atm", [128, 2, 128], BF16)
        o_n = sb("o_n", [128, G, 1024], BF16)
        onT = sb("onT", [128, 8, 128], BF16)
        s_sb = sb("s_sb", [128, G, 1024], BF16)
        zT_sb = sb("zT_sb", [128, 512], BF16)
        zm1 = sb("zm1", [128, 512], BF16)
        zdf = sb("zdf", [128, 512], BF16)
        L_sb = sb("L_sb", [128, 4, 8], F32)
        L2_sb = sb("L2_sb", [128, 8], F32)
        eq1_b = sb("eq1_b", [128, 8], BF16)
        eq1g = sb("eq1g", [128, 4, 8], BF16)
        eqdg = sb("eqdg", [128, 4, 8], BF16)
        eq2_b = sb("eq2_b", [128, 8], BF16)
        eqd_b = sb("eqd_b", [128, 8], BF16)
        eq1T = sb("eq1T", [8, 512], BF16)
        eqdT = sb("eqdT", [8, 512], BF16)
        vsum = sb("vsum", [128, G], F32)
        sc1 = sb("sc1", [128, 16], F32)
        rst8 = sb("rst8", [128, 8], F32)
        hs = sb("hs", [128, FBN, 512], BF16)
        wiring = sb("wiring", [128, 3, 1024], BF16)
        dnring = sb("dnring", [128, 3, 1024], BF16)
        r1r = sb("r1r", [128, 2, 512], BF16)
        t2r = sb("t2r", [128, 2, 512], BF16)
        g_sb = sb("g_sb", [128, 1024], F32)
        outr = sb("outr", [128, 2, 1024], F32)
        junk = sb("junk", [128, 1024], BF16)
        dbgsc = sb("dbgsc", [128, 64], F32)
        dbge = sb("dbge", [128, 32], F32)

        if DBG_DUMP:
            dsall = nc.dram_tensor("d_s_all", [T // TILE, 128, G, 1024], BF16,
                                   kind="ExternalOutput").ap()
            dgall = nc.dram_tensor("d_g_all", [T // TILE, 128, G, 1024], F32,
                                   kind="ExternalOutput").ap()
            dhall = nc.dram_tensor("d_hs_all", [T // TILE, 128, FBN, 512], BF16,
                                   kind="ExternalOutput").ap()
            dvall = nc.dram_tensor("d_vs_all", [T // TILE, 128, G], F32,
                                   kind="ExternalOutput").ap()

        wb_ev = [None, None]
        wb_ld = [None, None]
        wi_ld = [None] * 3
        wi_fr = [None] * 3
        dn_ld = [None] * 3
        dn_fr = [None] * 3
        x_ld = [f"t0xld{g}" for g in range(G)]
        last_store = [None, None]
        nwb = [0]

        def issue_wb(widx, half, slot, deps=()):
            nwb[0] += 1
            name = f"wb{nwb[0]}"
            d = list(deps) + [["cwq", "cwk", "cwv", "cwo"][widx]]
            if wb_ev[slot]:
                d.append(wb_ev[slot])
            P.op("sync", lambda y, w=widx, h=half, sl=slot: y.dma_start(
                out=wbuf[:, sl, :, :], in_=wT_s[w][:, :, h * 512:(h + 1) * 512]),
                 deps=d, dma=P.dma_sem(f"dWB{slot}"), event=name)
            wb_ld[slot] = name

        def issue_wi(t, fb):
            slot = fb % 3
            name = f"wi{t}_{fb}"
            d = [wi_fr[slot]] if wi_fr[slot] else ["cwi"]
            P.op("sync", lambda y, fb=fb, sl=slot: y.dma_start(
                out=wiring[:, sl, :], in_=wiT_s[fb]), deps=d, dma=P.dma_sem(f"dWI{slot}"), event=name)
            wi_ld[slot] = name

        def issue_dn(t, fb):
            slot = fb % 3
            name = f"dn{t}_{fb}"
            d = [dn_fr[slot]] if dn_fr[slot] else ["cdn"]
            P.op("sync", lambda y, fb=fb, sl=slot: y.dma_start(
                out=dnring[:, sl, :], in_=dnT_s[fb]), deps=d, dma=P.dma_sem(f"dDN{slot}"), event=name)
            dn_ld[slot] = name


        with P.block():
            P.op("vector", lambda v: v.memset(state[:], 0.0), event="st0")

        phase_blk = contextlib.nullcontext if MERGE_MAIN else P.block
        if MERGE_MAIN:
            P.open_block()
        def emit_A_dma(t):
            with phase_blk():
                for g in range(G):
                    if t > 0:
                        P.op("gpsimd", lambda y, g=g, t=t: y.dma_start(
                            out=x_sb[:, g, :], in_=xv[t, :, g, :]),
                             deps=[f"t{t-1}xn{g}", f"t{t-1}rstd{g}"], dma=P.dma_sem(f"dX{g}"),
                             event=f"t{t}xld{g}")
                    x_ld[g] = f"t{t}xld{g}"

        def emit_A(t):
            with phase_blk():
                for g in range(G):
                    emit_A_group(t, g)

        def emit_A_group(t, g):
            tn = f"t{t}"
            if True:
                if True:
                    gd = [x_ld[g]]
                    P.op("vector", lambda v, g=g: v.reduce_sum(sc1[:, g:g + 1], x_sb[:, g, :], axis=AX.X), deps=gd, fence=True)
                    P.op("vector", lambda v, g=g: v.tensor_scalar(
                        sc1[:, g:g + 1], sc1[:, g:g + 1], -1.0 / D, None, ALU.mult), event=f"{tn}nm{g}")
                    P.op("scalar", lambda s, g=g: s.activation(
                        junk[:], x_sb[:, g, :], AF.Square, bias=sc1[:, g:g + 1],
                        accum_out=sc1[:, 8 + g:9 + g]), deps=[f"{tn}nm{g}"] + gd)
                    P.op("scalar", lambda s, g=g: s.activation(
                        sc1[:, 8 + g:9 + g], sc1[:, 8 + g:9 + g], AF.Ln, scale=1.0 / D, bias=eps_c[:]))
                    P.op("scalar", lambda s, g=g: s.activation(
                        sc1[:, 8 + g:9 + g], sc1[:, 8 + g:9 + g], AF.Exp, scale=-0.5), event=f"{tn}rstd{g}")
                    xdeps = [f"{tn}rstd{g}"]
                    if t >= 1:
                        xdeps.append(f"t{t-1}xt{g}_pe")
                    P.op("vector", lambda v, g=g: v.tensor_scalar(
                        xng[:, g, :], x_sb[:, g, :], sc1[:, g:g + 1], sc1[:, 8 + g:9 + g],
                        ALU.add, ALU.mult), deps=xdeps, event=f"{tn}xn{g}")

        def emit_A_tr(t):
            tn = f"t{t}"
            tp = t % 2
            with phase_blk():
                for g in range(G):
                    trdeps = [f"{tn}xn{g}"]
                    if g >= 1:
                        trdeps += [f"{tn}xe{g-1}_0", f"{tn}xe{g-1}_1"]
                    elif t >= 1:
                        trdeps.append(f"t{t-1}fsi0")
                    else:
                        trdeps.append("p0gb")
                    for c in range(8):
                        b, q = c // 4, c % 4
                        ev = f"{tn}xt{g}_pe" if c == 7 else None
                        P.op("tensor", lambda tr, g=g, c=c, b=b, q=q: tr.transpose(
                            carve(b, q), xng[:, g, c * 128:(c + 1) * 128],
                            identb_r[:]), deps=trdeps if c == 0 else (), event=ev)
                    for b in range(2):
                        xedeps = [f"{tn}xt{g}_pe"]
                        P.op("scalar", lambda s, g=g, b=b, tp=tp: s.copy(
                            xnT[:, tp, b * 4:(b + 1) * 4, g * 128:(g + 1) * 128],
                            ps[:, b, 0:256].bitcast(BF16).rearrange("p (q n) -> p q n", q=4)),
                             deps=xedeps, event=f"{tn}xe{g}_{b}")

        for t in range(n_tiles):
            tn = f"t{t}"
            tp = t % 2
            xnT_done = [f"{tn}xe{g}_{b}" for g in range(G) for b in range(2)]
            if t == 0:
                emit_A(0)
                issue_wb(0, 0, 0)
                issue_wb(0, 1, 1)
            emit_A_tr(t)
            if t + 1 < n_tiles:
                emit_A_dma(t + 1)

            with phase_blk():
                for wi_, dstT, w1, w2, b1c, b2row in [
                        (0, qT, fm_r[0], fm_r[1], fbq1_r, fbq2_row),
                        (1, kT, fm_r[2], fm_r[3], fbk1_r, fbk2_row)]:
                    wn = f"{tn}w{wi_}"
                    pwn = f"{tn}w0"

                    def prev(h, suffix):
                        if h >= 2:
                            return [f"{wn}{suffix}{h-2}"]
                        if wi_ == 1:
                            return [f"{pwn}{suffix}{h+6}"]
                        if t >= 1:
                            return [f"t{t-1}w1{suffix}{h+6}"]
                        return []

                    for h in range(H):
                        half = h // 4
                        qb = h % 2
                        deps = [wb_ld[half]] + xnT_done + prev(h, "qe")
                        if wi_ == 0 and h < 2:
                            deps += [f"{tn}xe3_0", f"{tn}xe3_1"]
                        for c in range(8):
                            ev = f"{wn}q{h}" if c == 7 else None
                            P.op("tensor", lambda tr, h=h, c=c, qb=qb, half=half, tp=tp: tr.matmul(
                                bank(qb), wbuf[:, half, c, (h % 4) * 128:(h % 4 + 1) * 128],
                                xnT[:, tp, c, :], start=(c == 0), stop=(c == 7)),
                                 deps=deps if c == 0 else (), event=ev)
                        qed = [f"{wn}q{h}"] + prev(h, "fm")
                        P.op("vector", lambda v, h=h, qb=qb: v.tensor_copy(
                            q0T[:, h % 2, :], bank(qb)), deps=qed, event=f"{wn}qe{h}")
                        f1, f2 = 2 + h % 2, 4 + h % 2
                        d1 = [f"{wn}qe{h}"] + prev(h, "t1_")
                        if wi_ == 0 and h < 2 and t >= 1:
                            d1.append(f"t{t-1}fsi1")
                        P.op("tensor", lambda tr, h=h, f1=f1, w1=w1: tr.matmul(
                            bank(f1), w1[:], q0T[:, h % 2, :], start=True, stop=True), deps=d1)
                        d2 = [f"{wn}qe{h}"] + prev(h, "qT")
                        if wi_ == 0 and h < 2 and t >= 1:
                            d2.append(f"t{t-1}fsi2")
                        P.op("tensor", lambda tr, h=h, f2=f2, w2=w2: tr.matmul(
                            bank(f2), w2[:], q0T[:, h % 2, :], start=True, stop=False), deps=d2)
                        P.op("tensor", lambda tr, h=h, f2=f2, b2row=b2row: tr.matmul(
                            bank(f2), b2row[:], ones_r[:], start=False, stop=True),
                             event=f"{wn}fm{h}")
                        t1deps = [f"{wn}fm{h}"] + prev(h, "qT")
                        if wi_ == 0 and h < 2 and t >= 1:
                            t1deps.append(f"t{t-1}e{30 + h % 2}t2")
                        P.op("scalar", lambda s, h=h, f1=f1, b1c=b1c: s.activation(
                            scr512[:, h % 2, :], bank(f1), AF.Identity, bias=b1c[:]),
                             deps=t1deps, event=f"{wn}t1_{h}")
                        qTdeps = [f"{wn}t1_{h}"]
                        if t >= 1:
                            qTdeps.append(f"t{t-1}g3o{h}")
                            if wi_ == 1:
                                qTdeps.append(f"t{t-1}g3ktr{h // 4}")
                        P.op("vector", lambda v, h=h, f2=f2, dstT=dstT: v.tensor_tensor(
                            dstT[:, h, :], bank(f2), scr512[:, h % 2, :], ALU.mult),
                             deps=qTdeps, event=f"{wn}qT{h}")
                    wb_ev[0] = f"{wn}q3"
                    wb_ev[1] = f"{wn}q7"
                    if wi_ == 0:
                        issue_wb(1, 0, 0)
                        issue_wb(1, 1, 1)
                    else:
                        issue_wb(2, 0, 0)
                        issue_wb(2, 1, 1)

                vfree = [f"{tn}w1qe6", f"{tn}w1qe7", f"{tn}w1t1_6", f"{tn}w1t1_7"]
                for half in range(2):
                    for g in range(G):
                        vb = 2 * (g % 2) + half
                        deps = [wb_ld[half]] + vfree
                        if g >= 2:
                            deps.append(f"{tn}ve{g-2}_{half}")
                        for c in range(8):
                            ev = f"{tn}vm{g}_{half}" if c == 7 else None
                            P.op("tensor", lambda tr, g=g, c=c, vb=vb, half=half, tp=tp: tr.matmul(
                                bank(vb), xnT[:, tp, c, g * 128:(g + 1) * 128],
                                wbuf[:, half, c, :], start=(c == 0), stop=(c == 7)),
                                 deps=deps if c == 0 else (), event=ev)
                        vedeps = [f"{tn}vm{g}_{half}"]
                        if t >= 1:
                            vedeps += [f"t{t-1}g{g}kv0", f"t{t-1}g{g}kv1"]
                        P.op("vector", lambda v, g=g, half=half, vb=vb: v.tensor_copy(
                            v_sb[:, g, half * 512:(half + 1) * 512], bank(vb)),
                             deps=vedeps, event=f"{tn}ve{g}_{half}")
                wb_ev[0] = f"{tn}vm3_0"
                wb_ev[1] = f"{tn}vm3_1"
                issue_wb(3, 0, 0)
                issue_wb(3, 1, 1)

                zdeps = xnT_done + [f"{tn}w1qT6", f"{tn}w1qT7"]
                if t >= 1:
                    zdeps.append(f"t{t-1}fsi3")
                for c in range(8):
                    P.op("tensor", lambda tr, c=c, tp=tp: tr.matmul(
                        bank(6), aall_r[:, c * 128:(c + 1) * 128], xnT[:, tp, c, :],
                        start=(c == 0), stop=(c == 7)),
                         deps=zdeps if c == 0 else (), event=f"{tn}zmm" if c == 7 else None)
                zedeps = [f"{tn}zmm"]
                if t >= 1:
                    zedeps += [f"t{t-1}e{FBN-1}p1", f"t{t-1}e{FBN-1}pd"]
                P.op("vector", lambda v: v.tensor_copy(zT_sb[:], bank(6)), deps=zedeps,
                     event=f"{tn}ze")

                lg0deps = xnT_done + ([f"t{t-1}fsi3", f"t{t-1}m1mm", f"t{t-1}mdmm"] if t >= 1 else [])
                for g in range(G):
                    for c in range(8):
                        ev = f"{tn}lgall" if (g == G - 1 and c == 7) else None
                        P.op("tensor", lambda tr, g=g, c=c, tp=tp: tr.matmul(
                            ps[:, 7, g * 8:(g + 1) * 8], xnT[:, tp, c, g * 128:(g + 1) * 128],
                            gw_r[:, c * 8:(c + 1) * 8], start=(c == 0), stop=(c == 7)),
                             deps=lg0deps if (g == 0 and c == 0) else (), event=ev)
                Ldeps = [f"{tn}lgall", "p0gb"]
                if t >= 1:
                    Ldeps.append(f"t{t-1}gex3")
                P.op("vector", lambda v: v.tensor_tensor(
                    L_sb[:].rearrange("p a e -> p (a e)"), ps[:, 7, 0:32], gb_bc[:], ALU.add),
                     deps=Ldeps, event=f"{tn}Leall", fence=True)
            with phase_blk():
                for g in range(G):
                    cg = t * G + g
                    gn = f"{tn}g{g}"
                    pgn = f"{tn}g{g-1}" if g >= 1 else f"t{t-1}g{G-1}"
                    for bb in range(2):
                        tdeps = []
                        if cg >= 1:
                            tdeps.append(f"{pgn}tke{bb}")
                        if g == 0:
                            tdeps += [f"{tn}ze", f"{tn}Leall"]
                        for hh in range(4):
                            h = bb * 4 + hh
                            ev = f"{gn}ktr{bb}" if hh == 3 else None
                            P.op("tensor", lambda tr, h=h, bb=bb, hh=hh, g=g: tr.transpose(
                                carve(6 + bb, hh), kT[:, h, g * 128:(g + 1) * 128], identb_r[:]),
                                 deps=(tdeps + [f"{tn}w1qT{h}"]) if hh == 0 else (), event=ev)
                        P.op("vector", lambda v, bb=bb, g=g: v.tensor_copy(
                            ktok[:, bb * 4:(bb + 1) * 4, :],
                            ps[:, 6 + bb, 0:256].bitcast(BF16).rearrange("p (q n) -> p q n", q=4)),
                             deps=[f"{gn}ktr{bb}"], event=f"{gn}tke{bb}")
                    for h in range(H):
                        ob, ab = 2 + h % 2, 4 + h % 2
                        hp = h - 2 if h >= 2 else h + 6
                        pn = gn if h >= 2 else pgn
                        adeps = [f"{tn}w1qT{h}", f"{tn}w0qT{h}"]
                        if cg >= 1 or h >= 2:
                            adeps.append(f"{pn}atm{hp}")
                        if g == 0 and h < 2:
                            adeps += [f"{tn}w1qT6", f"{tn}w1qT7"]
                        P.op("tensor", lambda tr, h=h, g=g, ab=ab: tr.matmul(
                            bank(ab)[:, 0:128], kT[:, h, g * 128:(g + 1) * 128],
                            qT[:, h, g * 128:(g + 1) * 128], start=True, stop=True),
                             deps=adeps, event=f"{gn}A{h}")
                        amdeps = [f"{gn}A{h}"]
                        if cg >= 1 or h >= 2:
                            amdeps.append(f"{pn}o{hp}")
                        P.op("vector", lambda v, h=h, ab=ab: v.tensor_tensor(
                            atm[:, h % 2, :], bank(ab)[:, 0:128], mask_r[:], ALU.mult),
                             deps=amdeps, event=f"{gn}atm{h}")
                        odeps = []
                        if cg >= 1 or h >= 2:
                            odeps += [f"{pn}oe{hp}", f"{pn}sq{hp}"]
                        if g == 0 and h < 2:
                            odeps += [f"{tn}ve3_0", f"{tn}ve3_1"]
                        if cg > 0:
                            P.op("tensor", lambda tr, h=h, g=g, ob=ob: tr.matmul(
                                bank(ob)[:, 0:128], qT[:, h, g * 128:(g + 1) * 128],
                                sbf[:, h, :], start=True, stop=False),
                                 deps=odeps + [f"{pgn}cast{h}"])
                            P.op("tensor", lambda tr, h=h, g=g, ob=ob: tr.matmul(
                                bank(ob)[:, 0:128], atm[:, h % 2, :], v_sb[:, g, h * 128:(h + 1) * 128],
                                start=False, stop=True),
                                 deps=[f"{gn}atm{h}", f"{tn}ve{g}_{h // 4}"], event=f"{gn}o{h}")
                        else:
                            P.op("tensor", lambda tr, h=h, g=g, ob=ob: tr.matmul(
                                bank(ob)[:, 0:128], atm[:, h % 2, :], v_sb[:, g, h * 128:(h + 1) * 128],
                                start=True, stop=True),
                                 deps=odeps + [f"{gn}atm{h}", f"{tn}ve{g}_{h // 4}"], event=f"{gn}o{h}")
                        sqdeps = [f"{gn}o{h}"]
                        if cg >= 1:
                            sqdeps.append(f"{pgn}oe{h}")
                        P.op("scalar", lambda s, h=h, ob=ob: s.activation(
                            junk[:, 0:128], bank(ob)[:, 0:128], AF.Square,
                            accum_out=rst8[:, h:h + 1]), deps=sqdeps)
                        P.op("scalar", lambda s, h=h: s.activation(
                            rst8[:, h:h + 1], rst8[:, h:h + 1], AF.Ln, scale=1.0 / 128, bias=eps_c[:]))
                        P.op("scalar", lambda s, h=h: s.activation(
                            rst8[:, h:h + 1], rst8[:, h:h + 1], AF.Exp, scale=-0.5),
                             event=f"{gn}sq{h}")
                    
                        oedeps = [f"{gn}sq{h}"]
                        if t >= 1:
                            oedeps.append(f"t{t-1}d{g}otr{h // 4}")
                        P.op("vector", lambda v, h=h, g=g, ob=ob: v.tensor_scalar(
                            o_n[:, g, h * 128:(h + 1) * 128], bank(ob)[:, 0:128],
                            rst8[:, h:h + 1], None, ALU.mult), deps=oedeps, event=f"{gn}oe{h}")
                    for bb in range(2):
                        kdeps = [f"{gn}tke{bb}"]
                        if cg >= 1:
                            kdeps.append(f"{pgn}cast{bb * 4 + 3}")
                        if g == 0:
                            kdeps += [f"{tn}ve2_0", f"{tn}ve2_1"]
                        for hh in range(4):
                            h = bb * 4 + hh
                            ev = f"{gn}kv{bb}" if hh == 3 else None
                            P.op("tensor", lambda tr, h=h, hh=hh, bb=bb, g=g: tr.matmul(
                                ps[:, bb, hh * 128:(hh + 1) * 128], ktok[:, h, :],
                                v_sb[:, g, h * 128:(h + 1) * 128], start=(hh == 0), stop=(hh == 3),
                                skip_group_check=True),
                                 deps=kdeps if hh == 0 else (), event=ev)
                    for bb in range(2):
                        for hh in range(4):
                            h = bb * 4 + hh
                            adddeps = [f"{gn}kv{bb}"]
                            if cg >= 1:
                                adddeps.append(f"{pgn}cast{h}")
                            if cg == 0:
                                P.op("vector", lambda v, h=h, hh=hh, bb=bb: v.tensor_copy(
                                    state[:, h * 128:(h + 1) * 128], ps[:, bb, hh * 128:(hh + 1) * 128]),
                                     deps=adddeps, event=f"{gn}add{h}")
                            else:
                                P.op("vector", lambda v, h=h, hh=hh, bb=bb: v.tensor_tensor(
                                    state[:, h * 128:(h + 1) * 128], state[:, h * 128:(h + 1) * 128],
                                    ps[:, bb, hh * 128:(hh + 1) * 128], ALU.add),
                                     deps=adddeps, event=f"{gn}add{h}")
                            P.op("scalar", lambda s, h=h: s.copy(
                                sbf[:, h, :], state[:, h * 128:(h + 1) * 128]),
                                 deps=[f"{gn}add{h}", f"{gn}o{h}"], event=f"{gn}cast{h}")
                    if True:
                        Lg = L_sb[:, g, :]
                        P.op("vector", lambda v, Lg=Lg: v.reduce_max(sc1[:, 12:13], Lg, axis=AX.X), fence=True)
                        P.op("vector", lambda v, Lg=Lg: v.tensor_scalar(eq1_b[:], Lg, sc1[:, 12:13], None, ALU.is_equal),
                             fence=True)
                        P.op("vector", lambda v, Lg=Lg: v.scalar_tensor_tensor(
                            L2_sb[:], eq1_b[:], -1e30, Lg, ALU.mult, ALU.add), fence=True)
                        P.op("vector", lambda v: v.reduce_max(sc1[:, 13:14], L2_sb[:], axis=AX.X), fence=True)
                        P.op("vector", lambda v: v.tensor_scalar(eq2_b[:], L2_sb[:], sc1[:, 13:14], None, ALU.is_equal), fence=True)
                        P.op("vector", lambda v: v.tensor_tensor(eqd_b[:], eq2_b[:], eq1_b[:], ALU.subtract),
                             event=f"{tn}eqd{g}")
                        P.op("vector", lambda v: v.tensor_scalar(sc1[:, 14:15], sc1[:, 12:13], -1.0, None, ALU.mult),
                             event=f"{tn}gnm{g}")
                        P.op("scalar", lambda s, Lg=Lg: s.activation(L2_sb[:], Lg, AF.Exp,
                                                              bias=sc1[:, 14:15]),
                             deps=[f"{tn}gnm{g}", f"{tn}eqd{g}"], event=f"{tn}gex{g}")
                        P.op("vector", lambda v: v.reduce_sum(sc1[:, 15:16], L2_sb[:], axis=AX.X),
                             deps=[f"{tn}gex{g}"], fence=True)
                        P.op("vector", lambda v: v.tensor_tensor(sc1[:, 13:14], sc1[:, 13:14],
                                                                 sc1[:, 12:13], ALU.subtract),
                             event=f"{tn}gdm{g}")
                        P.op("scalar", lambda s: s.activation(sc1[:, 12:13], sc1[:, 13:14], AF.Exp),
                             deps=[f"{tn}gdm{g}"], event=f"{tn}ge2{g}")
                        P.op("vector", lambda v: v.reciprocal(sc1[:, 15:16], sc1[:, 15:16]), fence=True)
                        vsdeps = [f"{tn}ge2{g}"]
                        if t >= 1:
                            vsdeps.append(f"t{t-1}fsi{g}")
                        if DBG_DUMP and g == 0:
                            vsdeps.append(("dOUT", P.sems["dOUT"][1]))
                        P.op("vector", lambda v, g=g: v.scalar_tensor_tensor(
                            vsum[:, g:g + 1], sc1[:, 12:13], 1.0, sc1[:, 15:16], ALU.add, ALU.mult),
                             deps=vsdeps, event=f"{tn}vs{g}")
                        eqsdeps = [f"t{t-1}et1{g}", f"t{t-1}et2{g}"] if t >= 1 else []
                        P.op("vector", lambda v, g=g: v.tensor_copy(eq1g[:, g, :], eq1_b[:]),
                             deps=eqsdeps, event=f"{tn}eqs1{g}")
                        P.op("vector", lambda v, g=g: v.tensor_copy(eqdg[:, g, :], eqd_b[:]),
                             event=f"{tn}eqs2{g}")


            with phase_blk():
                for g in range(G):
                    tdeps = [f"{tn}eqs1{g}", f"{tn}eqs2{g}", f"{tn}g3atm7"]
                    if g >= 1:
                        tdeps += [f"{tn}ee1{g-1}", f"{tn}ee2{g-1}"]
                    P.op("tensor", lambda tr, g=g: tr.transpose(carve8(5, 0), eq1g[:, g, :], identb_r[:]),
                         deps=tdeps, event=f"{tn}et1{g}")
                    P.op("tensor", lambda tr, g=g: tr.transpose(carve8(5, 1), eqdg[:, g, :], identb_r[:]),
                         event=f"{tn}et2{g}")
                    eedeps0 = [f"{tn}et1{g}"]
                    if g == 0 and t >= 1:
                        eedeps0 += [f"t{t-1}m1mm", f"t{t-1}mdmm"]
                    P.op("scalar", lambda s, g=g: s.copy(eq1T[:, g * 128:(g + 1) * 128], carve8(5, 0)),
                         deps=eedeps0, event=f"{tn}ee1{g}")
                    P.op("scalar", lambda s, g=g: s.copy(eqdT[:, g * 128:(g + 1) * 128], car8 if False else carve8(5, 1)),
                         deps=[f"{tn}et2{g}"], event=f"{tn}ee2{g}")
                eedeps = [f"{tn}ee1{g}" for g in range(G)] + [f"{tn}ee2{g}" for g in range(G)]
                P.op("tensor", lambda tr: tr.matmul(bank(4), eexp_r[:], eq1T[:], start=True, stop=True),
                     deps=eedeps + [f"{tn}ze", f"{tn}g3atm6"], event=f"{tn}m1mm")
                zm1deps = [f"{tn}m1mm"]
                if t >= 1:
                    zm1deps.append(f"t{t-1}e{FBN-1}p1")
                P.op("vector", lambda v: v.tensor_tensor(zm1[:], bank(4), zT_sb[:], ALU.mult),
                     deps=zm1deps, event=f"{tn}zm1")
                P.op("tensor", lambda tr: tr.matmul(bank(5), eexp_r[:], eqdT[:], start=True, stop=True),
                     deps=eedeps, event=f"{tn}mdmm")
                zdfdeps = [f"{tn}mdmm"]
                if t >= 1:
                    zdfdeps.append(f"t{t-1}e{FBN-1}pd")
                P.op("vector", lambda v: v.tensor_tensor(zdf[:], bank(5), zT_sb[:], ALU.mult),
                     deps=zdfdeps, event=f"{tn}zdf")


            with phase_blk():
                for g in range(G):
                    gn = f"{tn}d{g}"
                    trdeps = [f"{tn}g{g}oe{h}" for h in range(H)]
                    if g >= 1:
                        trdeps += [f"{tn}d{g-1}oTe0", f"{tn}d{g-1}oTe1"]
                    else:
                        trdeps += [f"{tn}g3tke0", f"{tn}g3tke1"]
                    for c in range(8):
                        bb, q = c // 4, c % 4
                        ev = f"{gn}otr{bb}" if q == 3 else None
                        P.op("tensor", lambda tr, g=g, c=c, bb=bb, q=q: tr.transpose(
                            carve(6 + bb, q), o_n[:, g, c * 128:(c + 1) * 128], identb_r[:]),
                             deps=trdeps if c == 0 else (), event=ev)
                    for bb in range(2):
                        oTdeps = [f"{gn}otr{bb}"]
                        if g == 0 and t >= 1:
                            oTdeps += [f"t{t-1}d3sm0", f"t{t-1}d3sm1"]
                        P.op("vector", lambda v, bb=bb: v.tensor_scalar(
                            onT[:, bb * 4:(bb + 1) * 4, :],
                            ps[:, 6 + bb, 0:256].bitcast(BF16).rearrange("p (q n) -> p q n", q=4),
                            rms_r[:], None, ALU.mult),
                             deps=oTdeps, event=f"{gn}oTe{bb}")
                    for half in range(2):
                        sbk = 2 * (g % 2) + half
                        sdeps = [f"{gn}oTe0", f"{gn}oTe1", wb_ld[half]]
                        if g >= 2:
                            sdeps.append(f"{tn}d{g-2}se")
                        elif g == 0:
                            sdeps += [f"{tn}g3cast3", f"{tn}g3cast7",
                                      f"{tn}g3oe6", f"{tn}g3oe7", f"{tn}g3sq6", f"{tn}g3sq7"]
                        for c in range(8):
                            ev = f"{gn}sm{half}" if c == 7 else None
                            P.op("tensor", lambda tr, g=g, c=c, sbk=sbk, half=half: tr.matmul(
                                bank(sbk), onT[:, c, :], wbuf[:, half, c, :],
                                start=(c == 0), stop=(c == 7)),
                                 deps=sdeps if c == 0 else (), event=ev)
    
                    sedeps = [f"{gn}sm0", f"{gn}sm1"]
                    if t >= 1:
                        sedeps.append(f"t{t-1}fsg{g}")
                    if DBG_DUMP:
                        sedeps.append(("dOUT", P.sems["dOUT"][1]))
                    P.op("vector", lambda v, g=g: v.tensor_copy(
                        s_sb[:, g, :], ps[:, 2 * (g % 2):2 * (g % 2) + 2, :].rearrange("p a n -> p (a n)")),
                         deps=sedeps, event=f"{gn}se")
                wb_ev[0] = f"{tn}d3sm0"
                wb_ev[1] = f"{tn}d3sm1"
                for fb in range(3):
                    issue_wi(t, fb)
                if t + 1 < n_tiles:
                    issue_wb(0, 0, 0)
                    issue_wb(0, 1, 1)

            with phase_blk():
                for fb in range(FBN):
                    en = f"{tn}e{fb}"
                    slot = fb % 3
                    pb, db = 4 + fb % 2, 6 + fb % 2
                    pdeps = [wi_ld[slot]] + xnT_done
                    if fb >= 2:
                        pdeps += [f"{tn}e{fb-2}r1", f"{tn}e{fb-2}t2"]
                    for c in range(8):
                        P.op("tensor", lambda tr, fb=fb, c=c, pb=pb, slot=slot, tp=tp: tr.matmul(
                            bank(pb), wiring[:, slot, c * 128:(c + 1) * 128], xnT[:, tp, c, :],
                            start=(c == 0), stop=False), deps=pdeps if c == 0 else ())
                    P.op("tensor", lambda tr, fb=fb, pb=pb: tr.matmul(
                        bank(pb), ball_r[:, fb * 128:(fb + 1) * 128], zm1[:],
                        start=False, stop=True), deps=[f"{tn}zm1"], event=f"{en}p1")
                    wi_fr[slot] = f"{en}p1"
                    ddeps = [f"{tn}zdf"]
                    if fb >= 2:
                        ddeps += [f"{tn}e{fb-2}pdb"]
                    P.op("tensor", lambda tr, fb=fb, db=db: tr.matmul(
                        bank(db), ball_r[:, fb * 128:(fb + 1) * 128], zdf[:],
                        start=True, stop=True), deps=ddeps, event=f"{en}pd")
                    P.op("scalar", lambda s, fb=fb, pb=pb: s.activation(
                        r1r[:, fb % 2, :], bank(pb), AF.Relu, bias=bi_r[:, fb:fb + 1]),
                         deps=[f"{en}p1"] + ([f"{tn}e{fb-2}hs"] if fb >= 2 else []),
                         event=f"{en}r1")
                    P.op("scalar", lambda s, fb=fb, db=db: s.activation(
                        scr512[:, fb % 2, :], bank(db), AF.Identity, bias=bi_r[:, fb:fb + 1]),
                         deps=[f"{en}pd"] + ([f"{tn}e{fb-2}t2"] if fb >= 2 else []),
                         event=f"{en}pdb")
                    P.op("vector", lambda v, fb=fb, pb=pb: v.tensor_tensor(
                        t2r[:, fb % 2, :], bank(pb), scr512[:, fb % 2, :], ALU.add),
                         deps=[f"{en}p1", f"{en}pdb"], event=f"{en}t2", fence=True)
    
                    hsdeps = [f"{en}r1", f"{en}t2"]
                    if DBG_DUMP and fb == 0:
                        hsdeps.append(("dOUT", P.sems["dOUT"][1]))
                    P.op("vector", lambda v, fb=fb: v.scalar_tensor_tensor(
                        hs[:, fb, :], t2r[:, fb % 2, :], 0.0, r1r[:, fb % 2, :],
                        ALU.max, ALU.add), deps=hsdeps, event=f"{en}hs")
                    if fb + 3 < FBN:
                        issue_wi(t, fb + 3)
                for fb in range(3):
                    issue_dn(t, fb)
                for g in range(2):
                    xfdeps = [f"t{t-1}fout{g+2}"] if t >= 1 else []
                    P.op("gpsimd", lambda y, g=g, t=t: y.dma_start(
                        out=xF[:, g % 2, :], in_=xv[t, :, g, :]),
                         deps=xfdeps, dma=P.dma_sem(f"dXF{g % 2}"), event=f"{tn}xf{g}")

            if t + 1 < n_tiles:
                emit_A(t + 1)

            with phase_blk():
                fn_ = f"{tn}f"
                if DBG_DUMP:
                    for g in range(G):
                        P.op("sync", lambda y, g=g, t=t: y.dma_start(
                            out=dsall[t, :, g, :], in_=s_sb[:, g, :]),
                             deps=[f"{tn}d{g}se"], dma="dOUT")
                    P.op("sync", lambda y, t=t: y.dma_start(
                        out=dhall[t].rearrange("p f n -> p (f n)"),
                        in_=hs[:].rearrange("p f n -> p (f n)")),
                         deps=[f"{tn}e{FBN-1}hs", f"{tn}e{FBN-2}hs"], dma="dOUT")
                    P.op("sync", lambda y, t=t: y.dma_start(
                        out=dvall[t], in_=vsum[:]),
                         deps=[f"{tn}vs{g_}" for g_ in range(G)], dma="dOUT")
                idep = [f"{tn}e{FBN-2}r1", f"{tn}e{FBN-2}t2", f"{tn}e{FBN-1}r1", f"{tn}e{FBN-1}t2",
                        f"{tn}e{FBN-2}pdb", f"{tn}e{FBN-1}pdb", f"{tn}d2se", f"{tn}d3se"]
                for g in range(G):
                    for half in range(2):
                        yb = 2 * g + half
                        P.op("tensor", lambda tr, yb=yb, half=half: tr.matmul(
                            bank(yb), ones_r[0:1, 0:128], bo2_row[0:1, half * 512:(half + 1) * 512],
                            start=True, stop=False, skip_group_check=True),
                             deps=idep if yb == 0 else ())
                for fb in range(FBN):
                    slot = fb % 3
                    for g in range(G):
                        for half in range(2):
                            yb = 2 * g + half
                            last = fb == FBN - 1
                            ev = f"{fn_}fb{fb}" if (g == G - 1 and half == 1) else None
                            P.op("tensor", lambda tr, fb=fb, g=g, half=half, yb=yb, slot=slot, last=last: tr.matmul(
                                bank(yb), hs[:, fb, g * 128:(g + 1) * 128],
                                dnring[:, slot, half * 512:(half + 1) * 512],
                                start=False, stop=last, skip_group_check=True),
                                 deps=[dn_ld[slot], f"{tn}e{fb}hs"] if (g == 0 and half == 0) else (),
                                 event=ev)
                    dn_fr[slot] = f"{fn_}fb{fb}"
                    if fb + 3 < FBN:
                        issue_dn(t, fb + 3)
                for g in range(G):
                    sideps = [f"{fn_}fb{FBN-1}", f"{tn}vs{g}"]
                    if g >= 1:
                        sideps.append(f"{fn_}sg{g-1}")
                    elif t >= 1:
                        sideps.append(f"t{t-1}fsg3")
                    if DBG_DUMP and g >= 1:
                        sideps.append(("dOUT", P.sems["dOUT"][1]))
                    P.op("scalar", lambda s, g=g: s.activation(
                        g_sb[:], ps[:, 2 * g:2 * g + 2, :].rearrange("p a n -> p (a n)"),
                        AF.Silu, scale=vsum[:, g:g + 1]),
                         deps=sideps, event=f"{fn_}si{g}")
    
                    sgdeps = [f"{fn_}si{g}", f"{tn}d{g}se"]
                    if last_store[g % 2]:
                        sgdeps.append(last_store[g % 2])
                    P.op("vector", lambda v, g=g: v.tensor_tensor(
                        outr[:, g % 2, :], s_sb[:, g, :], g_sb[:], ALU.mult),
                         deps=sgdeps, event=f"{fn_}sg{g}", fence=True)
                    odeps = [f"{tn}xf{g}"]
                    P.op("vector", lambda v, g=g: v.tensor_tensor(
                        outr[:, g % 2, :], outr[:, g % 2, :], xF[:, g % 2, :], ALU.add),
                         deps=odeps, event=f"{fn_}out{g}")
                    if DBG_DUMP:
                        P.op("sync", lambda y, g=g, t=t: y.dma_start(
                            out=dgall[t, :, g, :], in_=g_sb[:]),
                             deps=[f"{fn_}sg{g}"], dma="dOUT")
                    P.op("gpsimd", lambda y, g=g, t=t: y.dma_start(
                        out=ov[t, :, g, :], in_=outr[:, g % 2, :]),
                         deps=[f"{fn_}out{g}"], dma=P.dma_sem(f"dO{g % 2}"), event=f"{fn_}st{g}")
                    last_store[g % 2] = f"{fn_}st{g}"
                    if g < 2:
                        P.op("gpsimd", lambda y, g=g, t=t: y.dma_start(
                            out=xF[:, g % 2, :], in_=xv[t, :, g + 2, :]),
                             deps=[f"{fn_}out{g}"], dma=P.dma_sem(f"dXF{g % 2}"),
                             event=f"{tn}xf{g+2}")
                    if False and t + 1 < n_tiles:
                        P.op("sync", lambda y, g=g, t=t: y.dma_start(
                            out=x_sb[:, g, :], in_=xv[t + 1, :, g, :]),
                             deps=[f"{fn_}out{g}"], dma="dLD", event=f"t{t+1}xld{g}")
                        x_ld[g] = f"t{t+1}xld{g}"

        if DBG_DUMP:
            dbg_specs = [("d_xnT", xnT), ("d_qT", qT), ("d_kT", kT), ("d_v", v_sb),
                         ("d_on", o_n), ("d_s", s_sb), ("d_zm1", zm1), ("d_zdf", zdf),
                         ("d_zT", zT_sb), ("d_state", state), ("d_vsum", vsum),
                         ("d_eq1T", eq1T), ("d_eqdT", eqdT), ("d_g", g_sb),
                         ("d_hs", hs), ("d_ktok", ktok), ("d_onT", onT), ("d_sbf", sbf),
                         ("d_L", L_sb), ("d_gb", gb_bc), ("d_sc1", sc1), ("d_eq1b", eq1_b), ("d_eqdb", eqd_b),
                         ("d_dbgsc", dbgsc), ("d_dbge", dbge)]
            with P.block():
                for nm, buf in dbg_specs:
                    ap = buf[:]
                    shp = list(ap.shape)
                    dts = ap.tensor.dtype
                    n = 1
                    for d in shp[1:]:
                        n *= d
                    dd = nc.dram_tensor(nm, [shp[0], n], dts, kind="ExternalOutput").ap()
                    if len(shp) == 3:
                        flat_sb = ap.rearrange("p a b -> p (a b)")
                    elif len(shp) == 4:
                        flat_sb = ap.rearrange("p a b c -> p (a b c)")
                    else:
                        flat_sb = ap
                    P.op("sync", lambda y, a=dd, b=flat_sb: y.dma_start(out=a, in_=b),
                         dma="dOUT")

        if MERGE_MAIN:
            P.close_block()

        with P.block():
            fdeps = [("dOUT", P.sems["dOUT"][1])]
            for snm in ("dO0", "dO1"):
                if snm in P.sems:
                    fdeps.append((snm, P.sems[snm][1]))
            P.op("sync", lambda y: y.nop(), deps=fdeps)

        P.build()
    return nc


def _host_layout(inputs):
    a = {k: np.ascontiguousarray(np.asarray(v, dtype=np.float32)) for k, v in inputs.items()}
    tileT = lambda w: np.ascontiguousarray(w.T.reshape(8, 128, w.shape[0]).transpose(1, 0, 2))
    ins = {}
    ins["wqT"] = tileT(a["L0_attn_Wq"])
    ins["wkT"] = tileT(a["L0_attn_Wk"])
    ins["wvT"] = tileT(a["L0_attn_Wv"])
    ins["woT"] = tileT(a["L0_attn_Wo"])
    ins["wiT"] = np.ascontiguousarray(
        a["L0_ffn_wi"].reshape(32, 128, 8, 128).transpose(0, 3, 2, 1).reshape(32, 128, 1024))
    ins["dnT"] = np.ascontiguousarray(a["L0_ffn_wo"].T.reshape(32, 128, 1024))
    ins["ballT"] = np.ascontiguousarray(a["L0_ffn_lora_B"].transpose(0, 2, 1).reshape(128, 4096))
    aallT = a["L0_ffn_lora_A"].reshape(128, 1024).T
    ins["aallT"] = np.ascontiguousarray(aallT.reshape(8, 128, 128).transpose(1, 0, 2).reshape(128, 1024))
    gwT = a["L0_ffn_gate_w"].T
    ins["gwT"] = np.ascontiguousarray(gwT.reshape(8, 128, 8).transpose(1, 0, 2).reshape(128, 64))
    ins["fmq1"] = np.ascontiguousarray(a["L0_attn_fmq_w1"].T)
    ins["fmq2"] = np.ascontiguousarray(a["L0_attn_fmq_w2"].T)
    ins["fmk1"] = np.ascontiguousarray(a["L0_attn_fmk_w1"].T)
    ins["fmk2"] = np.ascontiguousarray(a["L0_attn_fmk_w2"].T)
    ins["fbq1"] = a["L0_attn_fmq_b1"].reshape(128, 1)
    ins["fbk1"] = a["L0_attn_fmk_b1"].reshape(128, 1)
    ins["fbq2"] = a["L0_attn_fmq_b2"].reshape(1, 128)
    ins["fbk2"] = a["L0_attn_fmk_b2"].reshape(1, 128)
    ins["bi"] = np.ascontiguousarray(a["L0_ffn_bi"].reshape(32, 128).T)
    ins["bo"] = a["L0_ffn_bo"].reshape(1, 1024)
    ins["gb"] = np.tile(a["L0_ffn_gate_b"].reshape(1, 8), (1, 4))
    ins["rms"] = a["L0_attn_rms_w"].reshape(128, 1)
    ins["identb"] = np.eye(128, dtype=np.float32).astype(ml_dtypes.bfloat16)
    ins["mask"] = np.triu(np.ones((128, 128), dtype=np.float32))
    ee = np.zeros((8, 128), dtype=np.float32)
    for e in range(8):
        ee[e, e * 16:(e + 1) * 16] = 1.0
    ins["eexp"] = ee.astype(ml_dtypes.bfloat16)
    ins["ones"] = np.ones((1, 512), dtype=np.float32).astype(ml_dtypes.bfloat16)
    return ins


def kernel(**inputs):
    n_tiles = DBG_TILES or (T // TILE)
    cores = DBG_CORES or list(range(8))
    shared = _host_layout(inputs)
    x = np.ascontiguousarray(np.asarray(inputs["L0_attn_hidden_states"], dtype=np.float32))
    nc = build_nc(n_tiles)
    in_maps = [dict(shared, x=x[b]) for b in range(len(cores))]
    res = run_bass_kernel_spmd(nc, in_maps, core_ids=cores)
    out = np.stack([r["out"] for r in res.results], axis=0)
    if DBG_DUMP:
        kernel.dbg = res.results
    return out
